# version 22
# baseline (speedup 1.0000x reference)
import numpy as np
import ml_dtypes
import concourse.bass as bass
import concourse.mybir as mybir
from concourse.bass_utils import run_bass_kernel_spmd

F32 = mybir.dt.float32
BF16 = mybir.dt.bfloat16
AF = mybir.ActivationFunctionType
ALU = mybir.AluOpType
AX = mybir.AxisListType

S = 8192
H = 4096
NH = 32
HD = 128
E = 64
NCORE = 8
T = 1024
NHT = H // 128
SCALE = HD ** -0.5
CSAFE = 4.0
CUT = 115.0
SPAN = 56.0
EPS = 1e-5
GARBAGE_BIAS = -30000.0

SLOPES = [2.0 ** (-0.25 * (h + 1)) for h in range(NH)]


def _head_mode(h):
    s = SLOPES[h]
    if s * 128.0 > SPAN:
        return 'W64'
    for b in (8, 4, 2):
        if s * ((b - 1) * 1024 + 128) <= SPAN:
            return b
    return 1


def make_schedule():
    heads = []
    for h in range(NH):
        s = SLOPES[h]
        dmax = int((CUT / s + 127) // 128)
        mode = _head_mode(h)
        kts = []
        for kt in range(63, -1, -1):
            ja = max(0, -(-(kt - 7) // 8))
            jb = min(7, (kt + dmax - 7) // 8)
            if ja > jb:
                continue
            batches = []
            if mode == 'W64':
                for j in range(jb, ja - 1, -1):
                    for half in (0, 1):
                        c0 = j * 128 + half * 64
                        ref = (8 * j + 7) * 128 + (half + 1) * 64
                        batches.append((c0, c0 + 64, ref))
            else:
                B = mode
                for gg in reversed(range(8 // B)):
                    g0, g1 = gg * B, gg * B + B - 1
                    lo, hi = max(ja, g0), min(jb, g1)
                    if lo > hi:
                        continue
                    batches.append((lo * 128, (hi + 1) * 128,
                                    (8 * g1 + 7) * 128 + 128))
            diag = (kt % 8 == 7) and (ja <= (kt - 7) // 8 <= jb)
            kts.append((kt, ja, jb, tuple(batches), diag))
        heads.append((h, s, kts))
    return heads


SCHED = make_schedule()
HEAD_NV = [sum(len(b) for (_, _, _, b, _) in kts) for (_, _, kts) in SCHED]
HEAD_OFF = [0]
for nv in HEAD_NV:
    HEAD_OFF.append(HEAD_OFF[-1] + nv)
NVEC = HEAD_OFF[-1]
assert max(HEAD_NV) <= 160


def build_btab(core):
    tab = np.empty((128, NVEC), np.float32)
    p = np.arange(128, dtype=np.float64)
    vi = 0
    for (h, slope, kts) in SCHED:
        for (kt, ja, jb, batches, diag) in kts:
            for (c0, c1, ref) in batches:
                if kt < 7 - core:
                    tab[:, vi] = GARBAGE_BIAS
                else:
                    tab[:, vi] = (slope * (kt * 128 + p - ref) - CSAFE
                                  ).astype(np.float32)
                vi += 1
    assert vi == NVEC
    return tab


class Ctr:
    __slots__ = ("sem", "n")

    def __init__(self, sem):
        self.sem = sem
        self.n = 0

    def at(self):
        return (self.sem, self.n)


def _splits(a, b):
    out = []
    if b > 512:
        out.append((max(a, 512), b))
    if a < 512:
        out.append((a, min(b, 512)))
    return out


def build_program():
    nc = bass.Bass("TRN2", target_bir_lowering=False)

    def din(name, shape):
        return nc.dram_tensor(name, shape, F32, kind="ExternalInput").ap()

    xt = din("xt", [H, S])
    xo = din("xo", [H, T])
    wq = din("wq", [NH, NHT, 128, 128])
    wo = din("wo", [NHT, NHT, 128, 128])
    wkv = din("wkv", [NHT, 2, 128, 128])
    rwt = din("rwt", [NHT, 128, E])
    rbias = din("rbias", [E, 1])
    eb = din("eb", [NHT, E, 128])
    btab = din("btab", [128, NVEC])
    tri = din("tri", [128, 128])
    eye = din("eye", [128, 128])
    onescol = din("onescol", [128, 128])
    onesrow = din("onesrow", [1, 128])
    epsc = din("epsc", [1, 1])
    outT = nc.dram_tensor("outT", [H, T], F32, kind="ExternalOutput").ap()

    from contextlib import ExitStack
    with ExitStack() as ctx:
        en = ctx.enter_context
        s_xn = en(nc.sbuf_tensor("s_xn", [128, NHT * 1024], BF16))
        s_arena = en(nc.sbuf_tensor("s_arena", [128, NHT * 1024], BF16))
        s_kv = en(nc.sbuf_tensor("s_kv", [128, 16384], BF16))
        s_wbuf = en(nc.sbuf_tensor("s_wbuf", [128, 8192], BF16))
        s_pq = en(nc.sbuf_tensor("s_pq", [128, 3072], BF16))
        s_fin = en(nc.sbuf_tensor("s_fin", [128, 4096], BF16))
        s_bias = en(nc.sbuf_tensor("s_bias", [128, 320], F32))
        s_msc = en(nc.sbuf_tensor("s_msc", [64, 1024], BF16))
        s_wsb = en(nc.sbuf_tensor("s_wsb", [128, 1024], F32))
        s_rows = en(nc.sbuf_tensor("s_rows", [1, 1536], F32))
        s_rows_bf = en(nc.sbuf_tensor("s_rows_bf", [1, 1024], BF16))
        s_negS = en(nc.sbuf_tensor("s_negS", [1, 256], BF16))
        s_tri = en(nc.sbuf_tensor("s_tri", [128, 128], BF16))
        s_eyeb = en(nc.sbuf_tensor("s_eyeb", [128, 128], BF16))
        s_eye32 = en(nc.sbuf_tensor("s_eye32", [128, 128], F32))
        s_ones128 = en(nc.sbuf_tensor("s_ones128", [128, 128], BF16))
        s_onesrow = en(nc.sbuf_tensor("s_onesrow", [1, 128], BF16))
        s_onesrow32 = en(nc.sbuf_tensor("s_onesrow32", [1, 128], F32))
        s_rbias = en(nc.sbuf_tensor("s_rbias", [E, 1], F32))
        s_sm = en(nc.sbuf_tensor("s_sm", [128, 14], F32))

        ps_s0 = en(nc.psum_tensor("ps_s0", [128, 1024], F32))
        ps_s1 = en(nc.psum_tensor("ps_s1", [128, 1024], F32))
        ps_a = en(nc.psum_tensor("ps_a", [128, 512], F32))
        ps_b = en(nc.psum_tensor("ps_b", [128, 512], F32))
        ps_m0 = en(nc.psum_tensor("ps_m0", [128, 512], F32))
        ps_m1 = en(nc.psum_tensor("ps_m1", [128, 512], F32))

        sems = {}
        for nm in ("g", "d", "st", "tS", "tA", "tQ", "sE", "sM", "vE"):
            sems[nm] = en(nc.semaphore("sem_" + nm))
        G = Ctr(sems["g"]); D = Ctr(sems["d"]); ST = Ctr(sems["st"])
        TS = Ctr(sems["tS"]); TA = Ctr(sems["tA"]); TQ = Ctr(sems["tQ"])
        SE = Ctr(sems["sE"]); SM = Ctr(sems["sM"]); VE = Ctr(sems["vE"])

        PROG = {"sync": [], "gpsimd": [], "tensor": [], "scalar": [], "vector": []}

        def em(eng, fn, *args, waits=(), inc=None, **kw):
            rec_inc = None
            ret = None
            if inc is not None:
                ctr, amt = inc
                ctr.n += amt
                rec_inc = (ctr.sem, amt)
                ret = (ctr.sem, ctr.n)
            PROG[eng].append((fn, args, kw, tuple(w for w in waits if w), rec_inc))
            return ret

        pqf = s_pq[:, :].bitcast(F32)
        finf = s_fin[:, :].bitcast(F32)
        kvf = s_kv[:, :].bitcast(F32)
        sbiasb = s_bias[:, :].bitcast(BF16)
        wsbb = s_wsb[:, :].bitcast(BF16)
        ps_s1b = ps_s1[:, :].bitcast(BF16)
        ps_tpb = ps_s0[:, 640:896].bitcast(BF16)
        ps_bb = ps_b[:, :].bitcast(BF16)

        em("gpsimd", "dma_start", out=s_eyeb[:, :], in_=eye[:, :], inc=(G, 16))
        em("gpsimd", "dma_start", out=s_tri[:, :], in_=tri[:, :], inc=(G, 16))
        em("gpsimd", "dma_start", out=s_ones128[:, :], in_=onescol[:, :], inc=(G, 16))
        em("gpsimd", "dma_start", out=s_onesrow[:, :], in_=onesrow[:, :], inc=(G, 16))
        g_wkv = em("gpsimd", "dma_start",
                   out=s_wbuf.rearrange("p (a b j) -> p a b j", b=2, j=128),
                   in_=wkv.rearrange("a b p j -> p a b j"), inc=(G, 16))
        em("sync", "dma_start", out=s_eye32[:, :], in_=eye[:, :], inc=(D, 16))
        em("sync", "dma_start", out=s_onesrow32[:, :], in_=onesrow[:, :], inc=(D, 16))
        em("sync", "dma_start", out=s_rbias[:, :], in_=rbias[:, :], inc=(D, 16))
        d_consts = em("sync", "dma_start", out=s_sm[0:1, 3:4], in_=epsc[:, :],
                      inc=(D, 16))

        for jj in range(2):
            for ht in range(NHT):
                w = (g_wkv,) if (jj == 0 and ht == 0) else ()
                last = (jj == 1 and ht == NHT - 1)
                em("tensor", "matmul", ps_a[:, jj:jj + 1],
                   s_wbuf[:, ht * 256 + jj * 128:ht * 256 + jj * 128 + 128],
                   s_ones128[:, 0:1], start=(ht == 0), stop=(ht == NHT - 1),
                   waits=w, inc=(TQ, 1) if last else None)
        t_cs = TQ.at()
        v_cs = em("vector", "tensor_copy", sbiasb[:, 0:2], ps_a[:, 0:2],
                  waits=(t_cs,), inc=(VE, 1))
        for jj in range(2):
            em("tensor", "transpose", ps_bb[0:1, jj * 128:(jj + 1) * 128],
               sbiasb[:, jj:jj + 1], s_eyeb[:, :],
               waits=(v_cs,) if jj == 0 else (),
               inc=(TQ, 1) if jj == 1 else None)
        t_tp = TQ.at()
        for jj in range(2):
            em("scalar", "activation", s_negS[0:1, jj * 128:(jj + 1) * 128],
               ps_bb[0:1, jj * 128:(jj + 1) * 128], AF.Copy, scale=-1.0,
               waits=(t_tp,) if jj == 0 else (),
               inc=(SM, 1) if jj == 1 else None)
        s_negS_done = SM.at()

        NCH = 16
        c_mean = s_rows[0:1, 0:512]
        c_var = s_rows[0:1, 512:1024]
        c_rstd = s_rows[0:1, 1024:1536]
        r_meanb = s_rows_bf[0:1, 0:512]
        r_rstdb = s_rows_bf[0:1, 512:1024]
        chunk_mm_done = [None] * NCH
        chunk_sq_done = [None] * NCH
        chunk_ev_done = [None] * NCH

        def xt_tile(ci, ht):
            base = (ci % 2) * 16384 + (ht // 8) * 4096 + (ht % 8) * 512
            return s_arena[:, base:base + 512]

        xn_slot_done = [None] * 8
        pending_norm = []
        d_xo = None
        last_grp_t = None
        sq_grp_t = [None, None]
        stats_free = [None, None]
        kv_free = None
        pending_vt = []
        row_free = None

        def emit_norm(j):
            for ht in range(NHT):
                xns = s_xn[:, ht * 1024 + j * 128:ht * 1024 + (j + 1) * 128]
                w = (d_xo,) if ht == 0 else ()
                em("vector", "tensor_tensor", xns, xns,
                   pqf[:, j * 128:(j + 1) * 128], op=ALU.mult, waits=w)
                em("vector", "tensor_tensor", xns, xns,
                   finf[:, 1024 + j * 128:1024 + (j + 1) * 128],
                   op=ALU.subtract)
            em("vector", "tensor_copy", s_sm[:, 5:6], s_sm[:, 5:6],
               inc=(VE, 1))
            xn_slot_done[j] = VE.at()

        for ci in range(NCH):
            w0 = []
            if ci >= 2:
                w0 = [chunk_mm_done[ci - 2], chunk_sq_done[ci - 2]]
            for g4 in range(4):
                src = xt[g4 * 1024:(g4 + 1) * 1024, ci * 512:(ci + 1) * 512]
                src = src.rearrange("(ht p) t -> p ht t", p=128)
                dst = s_arena[:, (ci % 2) * 16384 + g4 * 4096:
                              (ci % 2) * 16384 + (g4 + 1) * 4096]
                dst = dst.rearrange("p (ht t) -> p ht t", t=512)
                em("gpsimd", "dma_start", out=dst, in_=src,
                   waits=tuple(w0) if g4 == 0 else (), inc=(G, 16))
            g_x = G.at()
            if ci == 1:
                for gg in range(2):
                    src = xo[gg * 2048:(gg + 1) * 2048, :].rearrange(
                        "(ht p) t -> p ht t", p=128)
                    dst = s_xn[:, gg * 16384:(gg + 1) * 16384].rearrange(
                        "p (ht t) -> p ht t", t=1024)
                    em("gpsimd", "dma_start", out=dst, in_=src, inc=(G, 16))
                d_xo = G.at()

            pmean = ps_m0 if ci % 2 == 0 else ps_s1[:, 0:512]
            psum2 = ps_m1 if ci % 2 == 0 else ps_s1[:, 512:1024]
            tw = [g_x]
            if stats_free[ci % 2] is not None:
                tw.append(stats_free[ci % 2])
            for ht in range(NHT):
                em("tensor", "matmul", pmean[:, :], s_ones128[:, :],
                   xt_tile(ci, ht), start=(ht == 0), stop=(ht == NHT - 1),
                   waits=tuple(tw) if ht == 0 else (),
                   inc=(TQ, 1) if ht == NHT - 1 else None)
            t_mean = TQ.at()

            mw = [t_mean]
            if row_free is not None:
                mw.append(row_free)
            s_mean = em("scalar", "activation", c_mean, pmean[0:1, :], AF.Copy,
                        scale=1.0 / H, waits=tuple(mw), inc=(SM, 1))
            v_meanb = em("vector", "tensor_copy", r_meanb, c_mean,
                         waits=(s_mean,), inc=(VE, 1))

            for grp in range(16):
                pair = grp % 2
                for k2 in range(2):
                    ht = grp * 2 + k2
                    w = []
                    if k2 == 0:
                        w = [g_x] if grp == 0 else []
                        if sq_grp_t[pair] is not None:
                            w.append(sq_grp_t[pair])
                    em("vector", "tensor_mul",
                       wsbb[:, (pair * 2 + k2) * 512:(pair * 2 + k2 + 1) * 512],
                       xt_tile(ci, ht), xt_tile(ci, ht),
                       waits=tuple(w), inc=(VE, 1) if k2 == 1 else None)
                v_sq = VE.at()
                for k2 in range(2):
                    ht = grp * 2 + k2
                    em("tensor", "matmul", psum2[:, :], s_ones128[:, :],
                       wsbb[:, (pair * 2 + k2) * 512:(pair * 2 + k2 + 1) * 512],
                       start=(ht == 0), stop=(ht == NHT - 1),
                       waits=(v_sq,) if k2 == 0 else (),
                       inc=(TQ, 1) if k2 == 1 else None)
                sq_grp_t[pair] = TQ.at()
            chunk_sq_done[ci] = VE.at()
            t_sumsq = sq_grp_t[15 % 2]

            for (cj, vkv_j, evd_j) in pending_vt:
                for kk in range(4):
                    em("tensor", "transpose", ps_tpb[:, kk * 128:(kk + 1) * 128],
                       sbiasb[:, kk * 128:(kk + 1) * 128], s_eyeb[:, :],
                       waits=(vkv_j, evd_j) if kk == 0 else (),
                       inc=(TQ, 1) if kk == 3 else None)
                t_vt = TQ.at()
                for kk in range(4):
                    em("vector", "tensor_copy",
                       s_kv[:, 8192 + (4 * cj + kk) * 128:
                            8192 + (4 * cj + kk + 1) * 128],
                       ps_tpb[:, kk * 128:(kk + 1) * 128],
                       waits=(t_vt,) if kk == 0 else ())
            pending_vt.clear()

            for jj, pdst in ((0, ps_a), (1, ps_b)):
                for ht in range(NHT):
                    w = []
                    if jj == 0 and ht == 0:
                        if kv_free is not None:
                            w.append(kv_free)
                        if ci == 0:
                            w.append(g_wkv)
                    em("tensor", "matmul", pdst[:, :],
                       s_wbuf[:, ht * 256 + jj * 128:ht * 256 + jj * 128 + 128],
                       xt_tile(ci, ht), start=(ht == 0), stop=False,
                       waits=tuple(w))
                em("tensor", "matmul", pdst[:, :],
                   s_negS[0:1, jj * 128:(jj + 1) * 128], r_meanb,
                   start=False, stop=True,
                   waits=(v_meanb, s_negS_done) if jj == 0 else (),
                   inc=(TQ, 1) if jj == 1 else None)
            t_kv = TQ.at()
            chunk_mm_done[ci] = t_kv

            em("scalar", "activation", c_var, psum2[0:1, :], AF.Copy,
               scale=1.0 / H, waits=(t_sumsq,))
            s_v1 = em("scalar", "activation", s_rows[0:1, 0:1],
                      s_rows[0:1, 0:1], AF.Copy, inc=(SM, 1))
            stats_free[ci % 2] = s_v1
            em("vector", "tensor_mul", c_rstd, c_mean, c_mean, waits=(s_mean,))
            v_v1 = em("vector", "tensor_sub", c_var, c_var, c_rstd,
                      waits=(s_v1,), inc=(VE, 1))
            em("scalar", "activation", c_rstd, c_var, AF.Ln,
               bias=s_sm[0:1, 3:4], waits=(v_v1, d_consts))
            s_rstd = em("scalar", "activation", c_rstd, c_rstd, AF.Exp,
                        scale=-0.5, inc=(SM, 1))
            v_rstdb = em("vector", "tensor_copy", r_rstdb, c_rstd,
                         waits=(s_rstd,), inc=(VE, 1))

            t_rb = em("tensor", "matmul", ps_s0[:, 0:512], s_onesrow[:, :],
                      r_rstdb, start=True, stop=True, waits=(v_rstdb,),
                      inc=(TQ, 1))
            v_rb = em("vector", "tensor_copy", finf[:, 0:512], ps_s0[:, 0:512],
                      waits=(t_rb,), inc=(VE, 1))

            em("vector", "tensor_tensor", s_kv[:, ci * 512:(ci + 1) * 512],
               ps_a[:, :], finf[:, 0:512], op=ALU.mult, waits=(t_kv,))
            v_kv = em("vector", "tensor_tensor", sbiasb[:, 0:512],
                      ps_b[:, :], finf[:, 0:512], op=ALU.mult, inc=(VE, 1))
            kv_free = v_kv

            pending_vt.append((ci, v_kv, None))

            for r in range(4):
                g4i = 4 * ci + r
                if g4i % 8 != 7:
                    continue
                j = g4i // 8
                v_ms = em("vector", "tensor_mul", c_var, c_mean, c_rstd,
                          inc=(VE, 1))
                t_ms = em("tensor", "matmul", ps_s0[:, 512:640],
                          s_onesrow32[:, :], c_var[0:1, r * 128:(r + 1) * 128],
                          start=True, stop=True, waits=(v_ms, d_consts),
                          inc=(TQ, 1))
                if ci == NCH - 1:
                    for ht in range(NHT):
                        xns = s_xn[:, ht * 1024 + j * 128:
                                   ht * 1024 + (j + 1) * 128]
                        w = (t_ms, d_xo) if ht == 0 else ()
                        em("vector", "tensor_tensor", xns, xns,
                           ps_s0[:, r * 128:(r + 1) * 128], op=ALU.mult,
                           waits=w)
                        em("vector", "tensor_tensor", xns, xns,
                           ps_s0[:, 512:640], op=ALU.subtract)
                    em("vector", "tensor_copy", s_sm[:, 5:6], s_sm[:, 5:6],
                       inc=(VE, 1))
                    xn_slot_done[j] = VE.at()
                else:
                    em("vector", "tensor_copy", pqf[:, j * 128:(j + 1) * 128],
                       finf[:, r * 128:(r + 1) * 128])
                    em("vector", "tensor_copy",
                       finf[:, 1024 + j * 128:1024 + (j + 1) * 128],
                       ps_s0[:, 512:640], waits=(t_ms,))
                    pending_norm.append((j, ci))
            chunk_ev_done[ci] = em("vector", "tensor_copy", s_sm[:, 5:6],
                                   s_sm[:, 5:6], inc=(VE, 1))
            row_free = chunk_ev_done[ci]
            pending_vt[-1] = (pending_vt[-1][0], pending_vt[-1][1],
                              chunk_ev_done[ci])
            keep = [p for p in pending_norm if p[1] > ci - 2]
            for j, _cc in pending_norm:
                if _cc <= ci - 2:
                    emit_norm(j)
            pending_norm[:] = keep

        for (cj, vkv_j, evd_j) in pending_vt:
            for kk in range(4):
                em("tensor", "transpose", ps_tpb[:, kk * 128:(kk + 1) * 128],
                   sbiasb[:, kk * 128:(kk + 1) * 128], s_eyeb[:, :],
                   waits=(vkv_j, evd_j, xn_slot_done[7]) if kk == 0 else (),
                   inc=(TQ, 1) if kk == 3 else None)
            t_vt = TQ.at()
            for kk in range(4):
                em("vector", "tensor_copy",
                   s_kv[:, 8192 + (4 * cj + kk) * 128:
                        8192 + (4 * cj + kk + 1) * 128],
                   ps_tpb[:, kk * 128:(kk + 1) * 128],
                   waits=(t_vt,) if kk == 0 else ())
        pending_vt.clear()
        for j, _cc in pending_norm:
            emit_norm(j)
        pending_norm.clear()
        p1_t_done = TQ.at()
        p1_v_done = VE.at()
        p1_s_done = SM.at()
        v_xn_all = p1_v_done

        v_xn = p1_v_done

        SU = [ps_s0[:, 0:512], ps_s0[:, 512:1024], ps_s1[:, 0:512]]
        ps_q = ps_s1[:, 512:1024]
        PU = [s_pq[:, u * 512:(u + 1) * 512] for u in range(3)]
        qtbuf = [s_pq[:, 1536:2560], s_fin[:, 2048:3072]]
        psq_free = [p1_s_done]
        qp_mm_done = [None] * NH
        qp_evac_done = [None] * NH

        class QF:

            def __init__(self, hq, g_wq):
                self.hq = hq
                self.g_wq = g_wq
                self.idx = 0
                self.ev0 = None

            def pop(self):
                if self.idx >= 64:
                    return False
                chunk, ht = self.idx // 32, self.idx % 32
                half = self.hq % 2
                w = []
                if self.idx == 0:
                    w = [self.g_wq, psq_free[0], v_xn]
                elif self.idx == 32:
                    w = [self.ev0]
                em("tensor", "matmul", ps_q,
                   s_wbuf[:, half * 4096 + ht * 128:half * 4096 + (ht + 1) * 128],
                   s_xn[:, ht * 1024 + chunk * 512:ht * 1024 + chunk * 512 + 512],
                   start=(ht == 0), stop=(ht == NHT - 1), waits=tuple(w),
                   inc=(TQ, 1) if ht == NHT - 1 else None)
                self.idx += 1
                if self.idx in (32, 64):
                    tq = TQ.at()
                    ve = em("vector", "tensor_copy",
                            qtbuf[self.hq % 2][:, chunk * 512:(chunk + 1) * 512],
                            ps_q, waits=(tq,), inc=(VE, 1))
                    if self.idx == 32:
                        self.ev0 = ve
                    else:
                        qp_mm_done[self.hq] = tq
                        qp_evac_done[self.hq] = ve
                        psq_free[0] = ve
                return True

        def mk_wq_dma(hq):
            half = hq % 2
            gw = [p1_t_done] if hq < 2 else [qp_mm_done[hq - 2]]
            return em("gpsimd", "dma_start",
                      out=s_wbuf[:, half * 4096:(half + 1) * 4096].rearrange(
                          "p (a d) -> p a d", d=128),
                      in_=wq[hq].rearrange("a p d -> p a d"),
                      waits=tuple(gw), inc=(G, 16))

        g_rwt = em("gpsimd", "dma_start",
                   out=s_pq[:, 0:2048].rearrange("p (a e) -> p a e", e=E),
                   in_=rwt.rearrange("a p e -> p a e"),
                   waits=(xn_slot_done[6],), inc=(G, 16))
        for chunk, pdst in ((0, ps_m0), (1, ps_m1)):
            for ht in range(NHT):
                w = (g_rwt, v_xn, p1_t_done, p1_s_done) if (chunk == 0 and ht == 0) else ()
                em("tensor", "matmul", pdst[0:64, :],
                   s_pq[:, ht * 64:(ht + 1) * 64],
                   s_xn[:, ht * 1024 + chunk * 512:ht * 1024 + chunk * 512 + 512],
                   start=(ht == 0), stop=(ht == NHT - 1), waits=w,
                   inc=(TQ, 1) if (chunk == 1 and ht == NHT - 1) else None)
        t_lg = TQ.at()
        em("vector", "tensor_scalar", finf[0:64, 0:512], ps_m0[0:64, :],
           s_rbias[:, :], None, ALU.add, waits=(t_lg, d_consts))
        s_lg = em("vector", "tensor_scalar", finf[0:64, 512:1024], ps_m1[0:64, :],
                  s_rbias[:, :], None, ALU.add, inc=(VE, 1))

        lgN = s_bias[:, 0:64]
        prb = s_bias[:, 64:128]
        mkb = s_bias[:, 128:192]
        mscN = sbiasb[:, 512:576]
        negmax = s_sm[:, 0:1]
        Zsum = s_sm[:, 4:5]
        Zr = s_sm[:, 1:2]
        wsumc = s_sm[:, 2:3]
        top8 = s_sm[:, 6:14]
        prev_v4 = None
        for ti in range(8):
            w = (s_lg,) if ti == 0 else (prev_v4,)
            t_1 = em("tensor", "transpose", ps_a[:, 0:64],
                     finf[0:64, ti * 128:(ti + 1) * 128], s_eye32[0:64, 0:64],
                     waits=w, inc=(TQ, 1))
            em("vector", "tensor_copy", lgN, ps_a[:, 0:64], waits=(t_1,))
            v_1 = em("vector", "reduce_max", negmax, lgN, axis=AX.X,
                     negate=True, inc=(VE, 1))
            s_1 = em("scalar", "activation", prb, lgN, AF.Exp, bias=negmax,
                     accum_out=Zsum, waits=(v_1,), inc=(SM, 1))
            em("vector", "max", out=top8, in_=prb, waits=(s_1,))
            em("vector", "reciprocal", Zr, Zsum)
            em("vector", "tensor_scalar", mkb, prb, top8[:, 7:8], None,
               ALU.is_ge)
            em("vector", "tensor_mul", mkb, mkb, prb)
            v_3 = em("vector", "tensor_scalar", mscN, mkb, Zr, 0.0, ALU.mult,
                     ALU.add, accum_out=wsumc, inc=(VE, 1))
            t_2a = em("tensor", "transpose", ps_bb[0:64, 0:128], mscN,
                      s_eyeb[:, :], waits=(v_3,), inc=(TQ, 1))
            t_2 = em("tensor", "transpose", ps_a[0:1, 64:192], wsumc,
                     s_eye32[:, :], inc=(TQ, 1))
            em("vector", "tensor_copy", s_msc[0:64, ti * 128:(ti + 1) * 128],
               ps_bb[0:64, 0:128], waits=(t_2,))
            prev_v4 = em("vector", "tensor_copy",
                         s_rows[0:1, ti * 128:(ti + 1) * 128],
                         ps_a[0:1, 64:192], inc=(VE, 1))
        em("tensor", "matmul", ps_s0[:, 0:512], s_onesrow32[:, :],
           s_rows[0:1, 0:512], start=True, stop=True, waits=(prev_v4,))
        t_wsb = em("tensor", "matmul", ps_s0[:, 512:1024], s_onesrow32[:, :],
                   s_rows[0:1, 512:1024], start=True, stop=True, inc=(TQ, 1))
        v_wsb = em("vector", "tensor_copy", s_wsb[:, :], ps_s0[:, :],
                   waits=(t_wsb,), inc=(VE, 1))

        unit_free = [v_wsb, v_wsb, p1_s_done]
        probs_free = [None, None, None]
        mm_free = [s_lg, s_lg]
        ab_free = [v_wsb, v_wsb]
        head_exp_done = [None] * NH
        ta_head_done = [None] * NH
        v_ev = [None] * NH
        tail_ev = [[None, None] for _ in range(NH)]

        qf = QF(0, mk_wq_dma(0))
        while qf.pop():
            pass
        next_qf = QF(1, mk_wq_dma(1))

        all_tasks = []
        meta = {}
        for (hh, slope, kts) in SCHED:
            tasks = []
            vi_base = 0
            for (kt, ja, jb, batches, diag) in kts:
                jd = (kt - 7) // 8 if diag else None
                for (a, b) in _splits(ja * 128, (jb + 1) * 128):
                    tb = []
                    for bi, (c0, c1, ref) in enumerate(batches):
                        lo_, hi_ = max(c0, a), min(c1, b)
                        if lo_ < hi_:
                            tb.append((lo_, hi_, vi_base + bi))
                    hd = jd if (jd is not None and a <= jd * 128 < b) else None
                    tasks.append((kt, a, b, tuple(tb), hd))
                vi_base += len(batches)
            assert vi_base == HEAD_NV[hh]
            lo_idx = [i for i, t in enumerate(tasks) if t[1] < 512]
            hi_idx = [i for i, t in enumerate(tasks) if t[1] >= 512]
            meta[hh] = dict(
                first_lo=min(lo_idx), last_lo=max(lo_idx),
                first_hi=min(hi_idx) if hi_idx else None,
                last_hi=max(hi_idx) if hi_idx else None,
                n=len(tasks))
            for i, t in enumerate(tasks):
                all_tasks.append((hh,) + t + (i,))

        NG = len(all_tasks)
        ts_g = [None] * NG
        se_g = [None] * NG
        vmk_g = [None] * NG
        d_bias_h = [None] * NH

        def ensure_bias(hh):
            if d_bias_h[hh] is None:
                half = hh % 2
                nv = HEAD_NV[hh]
                dw = [v_wsb] if hh < 2 else [head_exp_done[hh - 2]]
                d_bias_h[hh] = em("sync", "dma_start",
                                  out=s_bias[:, half * 160:half * 160 + nv],
                                  in_=btab[:, HEAD_OFF[hh]:HEAD_OFF[hh] + nv],
                                  waits=tuple(dw), inc=(D, 16))

        def emit_scores_g(g):
            hh, kt, a, b, tb, hd, loc = all_tasks[g]
            ensure_bias(hh)
            u = g % 3
            w = [unit_free[u]]
            if loc < 3:
                w.append(qp_evac_done[hh])
            ts_g[g] = em("tensor", "matmul", SU[u][:, 0:b - a],
                         s_kv[:, kt * 128:(kt + 1) * 128],
                         qtbuf[hh % 2][:, a:b],
                         start=True, stop=True, waits=tuple(w), inc=(TS, 1))

        def emit_exp_g(g):
            hh, kt, a, b, tb, hd, loc = all_tasks[g]
            u = g % 3
            half = hh % 2
            w = [ts_g[g], d_bias_h[hh]]
            if probs_free[u] is not None:
                w.append(probs_free[u])
            for bi, (c0, c1, vi) in enumerate(tb):
                em("scalar", "activation", PU[u][:, c0 - a:c1 - a],
                   SU[u][:, c0 - a:c1 - a], AF.Exp,
                   bias=s_bias[:, half * 160 + vi:half * 160 + vi + 1],
                   scale=SCALE,
                   waits=tuple(w) if bi == 0 else (),
                   inc=(SE, 1) if bi == len(tb) - 1 else None)
            se_g[g] = SE.at()
            unit_free[u] = se_g[g]

        def emit_mask_g(g):
            hh, kt, a, b, tb, hd, loc = all_tasks[g]
            if hd is None:
                return
            u = g % 3
            c = hd * 128 - a
            vmk_g[g] = em("vector", "tensor_mul", PU[u][:, c:c + 128],
                          PU[u][:, c:c + 128], s_tri[:, :],
                          waits=(se_g[g],), inc=(VE, 1))

        def emit_av_g(g):
            hh, kt, a, b, tb, hd, loc = all_tasks[g]
            u = g % 3
            m = meta[hh]
            lo = a < 512
            w = [se_g[g]]
            if vmk_g[g] is not None:
                w.append(vmk_g[g])
            first = (loc == m["first_lo"]) if lo else (loc == m["first_hi"])
            last = (loc == m["last_lo"]) if lo else (loc == m["last_hi"])
            if first:
                w.append(ab_free[0 if lo else 1])
                mf = mm_free[0 if lo else 1]
                if isinstance(mf, tuple) and isinstance(mf[0], tuple):
                    w.extend(mf)
                else:
                    w.append(mf)
            pd = ps_a if lo else ps_b
            pm = ps_m0 if lo else ps_m1
            em("tensor", "matmul", pd[:, a % 512:a % 512 + (b - a)],
               s_kv[:, 8192 + kt * 128:8192 + (kt + 1) * 128],
               PU[u][:, 0:b - a], start=first, stop=last, waits=tuple(w))
            em("tensor", "matmul", pm[:, a % 512:a % 512 + (b - a)],
               s_ones128[:, :], PU[u][:, 0:b - a],
               start=first, stop=last, inc=(TA, 1))
            probs_free[u] = TA.at()
            if last:
                emit_tail_bank(hh, 0 if lo else 1)

        def emit_tail_bank(hh, bank):
            ta_now = TA.at()
            pm = ps_m0 if bank == 0 else ps_m1
            pd = ps_a if bank == 0 else ps_b
            fr = finf[:, 0:512] if bank == 0 else finf[:, 512:1024]
            em("vector", "reciprocal", fr, pm[:, :], waits=(ta_now,))
            vr = em("vector", "tensor_tensor",
                    s_arena[:, hh * 1024 + bank * 512:
                            hh * 1024 + (bank + 1) * 512],
                    pd[:, :], fr, op=ALU.mult, inc=(VE, 1))
            ab_free[bank] = vr
            mm_free[bank] = vr
            tail_ev[hh][bank] = vr

        def emit_tail(hh):
            head_exp_done[hh] = SE.at()
            ta_head_done[hh] = TA.at()
            assert tail_ev[hh][0] is not None and tail_ev[hh][1] is not None
            v_ev[hh] = max(tail_ev[hh][0], tail_ev[hh][1],
                           key=lambda t: t[1])

        qf = next_qf
        next_qf = QF(2, mk_wq_dma(2))

        def lookahead(g):
            nonlocal qf, next_qf
            hh = all_tasks[g][0]
            cur = all_tasks[g - 3][0] if g >= 3 else all_tasks[0][0]
            if hh != cur and qf is not None and qf.hq == hh:
                while qf.pop():
                    pass
                qf = next_qf if next_qf is not None and next_qf.hq < NH else None
                if hh + 2 < NH:
                    next_qf = QF(hh + 2, mk_wq_dma(hh + 2))
                else:
                    next_qf = None
            emit_scores_g(g)
            emit_exp_g(g)
            emit_mask_g(g)

        for g in range(min(3, NG)):
            emit_scores_g(g)
            emit_exp_g(g)
            emit_mask_g(g)
        for g in range(NG):
            hh = all_tasks[g][0]
            loc = all_tasks[g][6]
            emit_av_g(g)
            if loc == meta[hh]["n"] - 1:
                emit_tail(hh)
            if g + 3 < NG:
                lookahead(g + 3)
            if qf is not None and qf.hq == hh + 1:
                if not qf.pop():
                    pass

        attn_done = ta_head_done[NH - 1]
        psfree = {"s0": unit_free[1], "s1": psq_free[0],
                  "a": v_ev[NH - 1], "b": v_ev[NH - 1]}
        p5_s0_extra = unit_free[0]
        p5_s1_extra = unit_free[2]
        qproj_done = qp_mm_done

        t_sT = [None] * NHT
        v_fin = [None] * NHT
        v_g2 = [None] * NHT
        g_wo_pre = [None, None]
        for jt in range(2):
            g_wo_pre[jt] = em("gpsimd", "dma_start",
                              out=s_wbuf[:, jt * 4096:(jt + 1) * 4096].rearrange(
                                  "p (a d) -> p a d", d=128),
                              in_=wo[jt].rearrange("a p d -> p a d"),
                              waits=(qproj_done[NH - 2 + jt],), inc=(G, 16))
        g_eb = em("gpsimd", "dma_start",
                  out=s_kv[0:64, 8192:12288].rearrange("p (a j) -> p a j", j=128),
                  in_=eb.rearrange("a p j -> p a j"),
                  waits=(attn_done,), inc=(G, 16))
        for jt in range(NHT):
            half = jt % 2
            if jt < 2:
                g_wo = g_wo_pre[jt]
            else:
                g_wo = em("gpsimd", "dma_start",
                          out=s_wbuf[:, half * 4096:(half + 1) * 4096].rearrange(
                              "p (a d) -> p a d", d=128),
                          in_=wo[jt].rearrange("a p d -> p a d"),
                          waits=(t_sT[jt - 2],), inc=(G, 16))
            dxw = [attn_done] if jt < 2 else [v_fin[jt - 2]]
            d_xo = em("sync", "dma_start",
                      out=kvf[:, half * 1024:(half + 1) * 1024],
                      in_=xo[jt * 128:(jt + 1) * 128, :],
                      waits=tuple(dxw), inc=(D, 16))
            psT = ps_s0 if half == 0 else ps_s1
            key = "s0" if half == 0 else "s1"
            for chunk in range(2):
                for dtt in range(NHT):
                    w = []
                    if chunk == 0 and dtt == 0:
                        w = [g_wo, psfree[key]]
                        if jt == 0:
                            w.append(v_ev[NH - 1])
                            w.append(p5_s0_extra)
                        if jt == 1:
                            w.append(p5_s1_extra)
                    em("tensor", "matmul",
                       psT[:, chunk * 512:(chunk + 1) * 512],
                       s_wbuf[:, half * 4096 + dtt * 128:half * 4096 + (dtt + 1) * 128],
                       s_arena[:, dtt * 1024 + chunk * 512:dtt * 1024 + chunk * 512 + 512],
                       start=(dtt == 0), stop=(dtt == NHT - 1), waits=tuple(w),
                       inc=(TQ, 1) if (chunk == 1 and dtt == NHT - 1) else None)
            t_sT[jt] = TQ.at()
            bw = [g_eb] if jt == 0 else [v_g2[jt - 1]]
            bw.append(psfree["a"])
            em("tensor", "matmul", ps_a[:, :],
               s_kv[0:64, 8192 + jt * 128:8192 + (jt + 1) * 128],
               s_msc[0:64, 0:512], start=True, stop=True, waits=tuple(bw))
            t_bm = em("tensor", "matmul", ps_b[:, :],
                      s_kv[0:64, 8192 + jt * 128:8192 + (jt + 1) * 128],
                      s_msc[0:64, 512:1024], start=True, stop=True,
                      inc=(TQ, 1))
            psfree["a"] = None

            g2 = finf[:, 0:1024]
            gt = finf[:, 1024:2048]
            w = [v_wsb] if jt == 0 else []
            em("vector", "tensor_tensor", g2, s_xn[:, jt * 1024:(jt + 1) * 1024],
               s_wsb[:, :], op=ALU.mult, waits=tuple(w))
            em("vector", "tensor_tensor", g2[:, 0:512], g2[:, 0:512],
               ps_a[:, :], op=ALU.add, waits=(t_bm,))
            v_g2[jt] = em("vector", "tensor_tensor", g2[:, 512:1024],
                          g2[:, 512:1024], ps_b[:, :], op=ALU.add,
                          inc=(VE, 1))
            psfree["a"] = v_g2[jt]
            s_si = em("scalar", "activation", gt, g2, AF.Silu,
                      waits=(v_g2[jt],), inc=(SM, 1))
            em("vector", "tensor_tensor", g2, psT[:, :], gt, op=ALU.mult,
               waits=(s_si, t_sT[jt]))
            v_fin[jt] = em("vector", "tensor_tensor",
                           kvf[:, 2048 + half * 1024:2048 + (half + 1) * 1024],
                           g2, kvf[:, half * 1024:(half + 1) * 1024],
                           op=ALU.add, waits=(d_xo,), inc=(VE, 1))
            psfree[key] = v_fin[jt]
            em("sync", "dma_start",
               out=outT[jt * 128:(jt + 1) * 128, :],
               in_=kvf[:, 2048 + half * 1024:2048 + (half + 1) * 1024],
               waits=(v_fin[jt],), inc=(ST, 16))
        em("sync", None, waits=(ST.at(),))

        block = en(nc.Block())

        def replay(engine, key):
            for fn, args, kw, waits, inc in PROG[key]:
                for sw in waits:
                    engine.wait_ge(sw[0], sw[1])
                if fn is None:
                    continue
                ins = getattr(engine, fn)(*args, **kw)
                if inc is not None:
                    ins.then_inc(inc[0], inc[1])

        @block.sync
        def _(engx):
            replay(engx, "sync")

        @block.gpsimd
        def _(engx):
            replay(engx, "gpsimd")

        @block.tensor
        def _(engx):
            replay(engx, "tensor")

        @block.scalar
        def _(engx):
            replay(engx, "scalar")

        @block.vector
        def _(engx):
            replay(engx, "vector")

    return nc


_NC_CACHE = None


def kernel(L0_attn_hidden_states, L0_attn_Wq, L0_attn_Wkv, L0_attn_Wo,
           L0_ffn_router_weight, L0_ffn_router_bias, L0_ffn_expert_bias):
    global _NC_CACHE
    x = np.ascontiguousarray(np.asarray(L0_attn_hidden_states, np.float32)
                             .reshape(S, H))
    Wq = np.asarray(L0_attn_Wq, np.float32)
    Wkv = np.asarray(L0_attn_Wkv, np.float32)
    Wo = np.asarray(L0_attn_Wo, np.float32)
    Rw = np.asarray(L0_ffn_router_weight, np.float32)
    Rb = np.asarray(L0_ffn_router_bias, np.float32)
    Eb = np.asarray(L0_ffn_expert_bias, np.float32)

    wq_t = np.ascontiguousarray(
        Wq.reshape(NHT, 128, NH, 128).transpose(2, 0, 1, 3))
    wo_t = np.ascontiguousarray(
        Wo.reshape(NHT, 128, NHT, 128).transpose(2, 0, 1, 3))
    wkv_t = np.ascontiguousarray(
        Wkv.reshape(NHT, 128, 2, 128).transpose(0, 2, 1, 3))
    rwt_t = np.ascontiguousarray(Rw.T.reshape(NHT, 128, E))
    eb_t = np.ascontiguousarray(Eb.reshape(E, NHT, 128).transpose(1, 0, 2))
    rb_t = np.ascontiguousarray(Rb.reshape(E, 1))
    tri = np.ascontiguousarray(
        np.tril(np.ones((128, 128), np.float32)).T)
    eye = np.eye(128, dtype=np.float32)
    onescol = np.ones((128, 128), np.float32)
    onesrow = np.ones((1, 128), np.float32)

    xT = x.T
    in_maps = []
    own_rows = []
    for c in range(NCORE):
        idx = ((8 * np.arange(8)[:, None] + c) * 128
               + np.arange(128)[None, :]).reshape(-1)
        own_rows.append(idx)
        shift = (7 - c) * 128
        xt_c = np.zeros((H, S), np.float32)
        xt_c[:, shift:] = xT[:, :S - shift]
        xo_c = np.ascontiguousarray(xT[:, idx])
        in_maps.append(dict(
            xt=xt_c, xo=xo_c, wq=wq_t, wo=wo_t, wkv=wkv_t, rwt=rwt_t,
            rbias=rb_t, eb=eb_t, btab=build_btab(c), tri=tri, eye=eye,
            onescol=onescol, onesrow=onesrow,
            epsc=np.full((1, 1), EPS, np.float32)))

    if _NC_CACHE is None:
        _NC_CACHE = build_program()
    nc = _NC_CACHE
    res = run_bass_kernel_spmd(nc, in_maps, core_ids=list(range(NCORE)))

    out = np.empty((S, H), np.float32)
    for c in range(NCORE):
        out[own_rows[c]] = res.results[c]["outT"].T
    return out.reshape(1, S, H)


# version 24
# speedup vs baseline: 1.0066x; 1.0066x over previous
import numpy as np
import ml_dtypes
import concourse.bass as bass
import concourse.mybir as mybir
from concourse.bass_utils import run_bass_kernel_spmd

F32 = mybir.dt.float32
BF16 = mybir.dt.bfloat16
AF = mybir.ActivationFunctionType
ALU = mybir.AluOpType
AX = mybir.AxisListType

S = 8192
H = 4096
NH = 32
HD = 128
E = 64
NCORE = 8
T = 1024
NHT = H // 128
SCALE = HD ** -0.5
CSAFE = 4.0
CUT = 110.0
SPAN = 56.0
EPS = 1e-5
GARBAGE_BIAS = -30000.0

SLOPES = [2.0 ** (-0.25 * (h + 1)) for h in range(NH)]


def _head_mode(h):
    s = SLOPES[h]
    if s * 128.0 > SPAN:
        return 'W64'
    for b in (8, 4, 2):
        if s * ((b - 1) * 1024 + 128) <= SPAN:
            return b
    return 1


def make_schedule():
    heads = []
    for h in range(NH):
        s = SLOPES[h]
        dmax = int((CUT / s + 127) // 128)
        mode = _head_mode(h)
        kts = []
        for kt in range(63, -1, -1):
            ja = max(0, -(-(kt - 7) // 8))
            jb = min(7, (kt + dmax - 7) // 8)
            if ja > jb:
                continue
            batches = []
            if mode == 'W64':
                for j in range(jb, ja - 1, -1):
                    for half in (0, 1):
                        c0 = j * 128 + half * 64
                        ref = (8 * j + 7) * 128 + (half + 1) * 64
                        batches.append((c0, c0 + 64, ref))
            else:
                B = mode
                for gg in reversed(range(8 // B)):
                    g0, g1 = gg * B, gg * B + B - 1
                    lo, hi = max(ja, g0), min(jb, g1)
                    if lo > hi:
                        continue
                    batches.append((lo * 128, (hi + 1) * 128,
                                    (8 * g1 + 7) * 128 + 128))
            diag = (kt % 8 == 7) and (ja <= (kt - 7) // 8 <= jb)
            kts.append((kt, ja, jb, tuple(batches), diag))
        heads.append((h, s, kts))
    return heads


SCHED = make_schedule()
HEAD_NV = [sum(len(b) for (_, _, _, b, _) in kts) for (_, _, kts) in SCHED]
HEAD_OFF = [0]
for nv in HEAD_NV:
    HEAD_OFF.append(HEAD_OFF[-1] + nv)
NVEC = HEAD_OFF[-1]
assert max(HEAD_NV) <= 160


def build_btab(core):
    tab = np.empty((128, NVEC), np.float32)
    p = np.arange(128, dtype=np.float64)
    vi = 0
    for (h, slope, kts) in SCHED:
        for (kt, ja, jb, batches, diag) in kts:
            for (c0, c1, ref) in batches:
                if kt < 7 - core:
                    tab[:, vi] = GARBAGE_BIAS
                else:
                    tab[:, vi] = (slope * (kt * 128 + p - ref) - CSAFE
                                  ).astype(np.float32)
                vi += 1
    assert vi == NVEC
    return tab


class Ctr:
    __slots__ = ("sem", "n")

    def __init__(self, sem):
        self.sem = sem
        self.n = 0

    def at(self):
        return (self.sem, self.n)


def _splits(a, b):
    out = []
    if b > 512:
        out.append((max(a, 512), b))
    if a < 512:
        out.append((a, min(b, 512)))
    return out


def build_program():
    nc = bass.Bass("TRN2", target_bir_lowering=False)

    def din(name, shape):
        return nc.dram_tensor(name, shape, F32, kind="ExternalInput").ap()

    xt = din("xt", [H, S])
    xo = din("xo", [H, T])
    wq = din("wq", [NH, NHT, 128, 128])
    wo = din("wo", [NHT, NHT, 128, 128])
    wkv = din("wkv", [NHT, 2, 128, 128])
    rwt = din("rwt", [NHT, 128, E])
    rbias = din("rbias", [E, 1])
    eb = din("eb", [NHT, E, 128])
    btab = din("btab", [128, NVEC])
    tri = din("tri", [128, 128])
    eye = din("eye", [128, 128])
    onescol = din("onescol", [128, 128])
    onesrow = din("onesrow", [1, 128])
    epsc = din("epsc", [1, 1])
    outT = nc.dram_tensor("outT", [H, T], F32, kind="ExternalOutput").ap()

    from contextlib import ExitStack
    with ExitStack() as ctx:
        en = ctx.enter_context
        s_xn = en(nc.sbuf_tensor("s_xn", [128, NHT * 1024], BF16))
        s_arena = en(nc.sbuf_tensor("s_arena", [128, NHT * 1024], BF16))
        s_kv = en(nc.sbuf_tensor("s_kv", [128, 16384], BF16))
        s_wbuf = en(nc.sbuf_tensor("s_wbuf", [128, 8192], BF16))
        s_pq = en(nc.sbuf_tensor("s_pq", [128, 3072], BF16))
        s_fin = en(nc.sbuf_tensor("s_fin", [128, 4096], BF16))
        s_bias = en(nc.sbuf_tensor("s_bias", [128, 320], F32))
        s_msc = en(nc.sbuf_tensor("s_msc", [64, 1024], BF16))
        s_wsb = en(nc.sbuf_tensor("s_wsb", [128, 1024], F32))
        s_rows = en(nc.sbuf_tensor("s_rows", [1, 1536], F32))
        s_rows_bf = en(nc.sbuf_tensor("s_rows_bf", [1, 1024], BF16))
        s_negS = en(nc.sbuf_tensor("s_negS", [1, 256], BF16))
        s_tri = en(nc.sbuf_tensor("s_tri", [128, 128], BF16))
        s_eyeb = en(nc.sbuf_tensor("s_eyeb", [128, 128], BF16))
        s_eye32 = en(nc.sbuf_tensor("s_eye32", [128, 128], F32))
        s_ones128 = en(nc.sbuf_tensor("s_ones128", [128, 128], BF16))
        s_onesrow = en(nc.sbuf_tensor("s_onesrow", [1, 128], BF16))
        s_onesrow32 = en(nc.sbuf_tensor("s_onesrow32", [1, 128], F32))
        s_rbias = en(nc.sbuf_tensor("s_rbias", [E, 1], F32))
        s_sm = en(nc.sbuf_tensor("s_sm", [128, 14], F32))

        ps_s0 = en(nc.psum_tensor("ps_s0", [128, 1024], F32))
        ps_s1 = en(nc.psum_tensor("ps_s1", [128, 1024], F32))
        ps_a = en(nc.psum_tensor("ps_a", [128, 512], F32))
        ps_b = en(nc.psum_tensor("ps_b", [128, 512], F32))
        ps_m0 = en(nc.psum_tensor("ps_m0", [128, 512], F32))
        ps_m1 = en(nc.psum_tensor("ps_m1", [128, 512], F32))

        sems = {}
        for nm in ("g", "d", "st", "tS", "tA", "tQ", "sE", "sM", "vE"):
            sems[nm] = en(nc.semaphore("sem_" + nm))
        G = Ctr(sems["g"]); D = Ctr(sems["d"]); ST = Ctr(sems["st"])
        TS = Ctr(sems["tS"]); TA = Ctr(sems["tA"]); TQ = Ctr(sems["tQ"])
        SE = Ctr(sems["sE"]); SM = Ctr(sems["sM"]); VE = Ctr(sems["vE"])

        PROG = {"sync": [], "gpsimd": [], "tensor": [], "scalar": [], "vector": []}

        def em(eng, fn, *args, waits=(), inc=None, **kw):
            rec_inc = None
            ret = None
            if inc is not None:
                ctr, amt = inc
                ctr.n += amt
                rec_inc = (ctr.sem, amt)
                ret = (ctr.sem, ctr.n)
            PROG[eng].append((fn, args, kw, tuple(w for w in waits if w), rec_inc))
            return ret

        pqf = s_pq[:, :].bitcast(F32)
        finf = s_fin[:, :].bitcast(F32)
        kvf = s_kv[:, :].bitcast(F32)
        sbiasb = s_bias[:, :].bitcast(BF16)
        wsbb = s_wsb[:, :].bitcast(BF16)
        ps_s1b = ps_s1[:, :].bitcast(BF16)
        ps_tpb = ps_s0[:, 640:896].bitcast(BF16)
        ps_bb = ps_b[:, :].bitcast(BF16)

        em("gpsimd", "dma_start", out=s_eyeb[:, :], in_=eye[:, :], inc=(G, 16))
        em("gpsimd", "dma_start", out=s_tri[:, :], in_=tri[:, :], inc=(G, 16))
        em("gpsimd", "dma_start", out=s_ones128[:, :], in_=onescol[:, :], inc=(G, 16))
        em("gpsimd", "dma_start", out=s_onesrow[:, :], in_=onesrow[:, :], inc=(G, 16))
        g_wkv = em("gpsimd", "dma_start",
                   out=s_wbuf.rearrange("p (a b j) -> p a b j", b=2, j=128),
                   in_=wkv.rearrange("a b p j -> p a b j"), inc=(G, 16))
        em("sync", "dma_start", out=s_eye32[:, :], in_=eye[:, :], inc=(D, 16))
        em("sync", "dma_start", out=s_onesrow32[:, :], in_=onesrow[:, :], inc=(D, 16))
        em("sync", "dma_start", out=s_rbias[:, :], in_=rbias[:, :], inc=(D, 16))
        d_consts = em("sync", "dma_start", out=s_sm[0:1, 3:4], in_=epsc[:, :],
                      inc=(D, 16))

        for jj in range(2):
            for ht in range(NHT):
                w = (g_wkv,) if (jj == 0 and ht == 0) else ()
                last = (jj == 1 and ht == NHT - 1)
                em("tensor", "matmul", ps_a[:, jj:jj + 1],
                   s_wbuf[:, ht * 256 + jj * 128:ht * 256 + jj * 128 + 128],
                   s_ones128[:, 0:1], start=(ht == 0), stop=(ht == NHT - 1),
                   waits=w, inc=(TQ, 1) if last else None)
        t_cs = TQ.at()
        v_cs = em("vector", "tensor_copy", sbiasb[:, 0:2], ps_a[:, 0:2],
                  waits=(t_cs,), inc=(VE, 1))
        for jj in range(2):
            em("tensor", "transpose", ps_bb[0:1, jj * 128:(jj + 1) * 128],
               sbiasb[:, jj:jj + 1], s_eyeb[:, :],
               waits=(v_cs,) if jj == 0 else (),
               inc=(TQ, 1) if jj == 1 else None)
        t_tp = TQ.at()
        for jj in range(2):
            em("scalar", "activation", s_negS[0:1, jj * 128:(jj + 1) * 128],
               ps_bb[0:1, jj * 128:(jj + 1) * 128], AF.Copy, scale=-1.0,
               waits=(t_tp,) if jj == 0 else (),
               inc=(SM, 1) if jj == 1 else None)
        s_negS_done = SM.at()

        NCH = 16
        c_mean = s_rows[0:1, 0:512]
        c_var = s_rows[0:1, 512:1024]
        c_rstd = s_rows[0:1, 1024:1536]
        r_meanb = s_rows_bf[0:1, 0:512]
        r_rstdb = s_rows_bf[0:1, 512:1024]
        chunk_mm_done = [None] * NCH
        chunk_sq_done = [None] * NCH
        chunk_ev_done = [None] * NCH

        def xt_tile(ci, ht):
            base = (ci % 2) * 16384 + (ht // 8) * 4096 + (ht % 8) * 512
            return s_arena[:, base:base + 512]

        xn_slot_done = [None] * 8
        pending_norm = []
        d_xo = None
        last_grp_t = None
        sq_grp_t = [None, None]
        stats_free = [None, None]
        kv_free = None
        pending_vt = []
        row_free = None

        def emit_norm(j):
            for ht in range(NHT):
                xns = s_xn[:, ht * 1024 + j * 128:ht * 1024 + (j + 1) * 128]
                w = (d_xo,) if ht == 0 else ()
                em("vector", "tensor_tensor", xns, xns,
                   pqf[:, j * 128:(j + 1) * 128], op=ALU.mult, waits=w)
                em("vector", "tensor_tensor", xns, xns,
                   finf[:, 1024 + j * 128:1024 + (j + 1) * 128],
                   op=ALU.subtract)
            em("vector", "tensor_copy", s_sm[:, 5:6], s_sm[:, 5:6],
               inc=(VE, 1))
            xn_slot_done[j] = VE.at()

        for ci in range(NCH):
            w0 = []
            if ci >= 2:
                w0 = [chunk_mm_done[ci - 2], chunk_sq_done[ci - 2]]
            for g4 in range(4):
                src = xt[g4 * 1024:(g4 + 1) * 1024, ci * 512:(ci + 1) * 512]
                src = src.rearrange("(ht p) t -> p ht t", p=128)
                dst = s_arena[:, (ci % 2) * 16384 + g4 * 4096:
                              (ci % 2) * 16384 + (g4 + 1) * 4096]
                dst = dst.rearrange("p (ht t) -> p ht t", t=512)
                em("gpsimd", "dma_start", out=dst, in_=src,
                   waits=tuple(w0) if g4 == 0 else (), inc=(G, 16))
            g_x = G.at()
            if ci == 1:
                for gg in range(2):
                    src = xo[gg * 2048:(gg + 1) * 2048, :].rearrange(
                        "(ht p) t -> p ht t", p=128)
                    dst = s_xn[:, gg * 16384:(gg + 1) * 16384].rearrange(
                        "p (ht t) -> p ht t", t=1024)
                    em("gpsimd", "dma_start", out=dst, in_=src, inc=(G, 16))
                d_xo = G.at()

            pmean = ps_m0 if ci % 2 == 0 else ps_s1[:, 0:512]
            psum2 = ps_m1 if ci % 2 == 0 else ps_s1[:, 512:1024]
            tw = [g_x]
            if stats_free[ci % 2] is not None:
                tw.append(stats_free[ci % 2])
            for ht in range(NHT):
                em("tensor", "matmul", pmean[:, :], s_ones128[:, :],
                   xt_tile(ci, ht), start=(ht == 0), stop=(ht == NHT - 1),
                   waits=tuple(tw) if ht == 0 else (),
                   inc=(TQ, 1) if ht == NHT - 1 else None)
            t_mean = TQ.at()

            mw = [t_mean]
            if row_free is not None:
                mw.append(row_free)
            s_mean = em("scalar", "activation", c_mean, pmean[0:1, :], AF.Copy,
                        scale=1.0 / H, waits=tuple(mw), inc=(SM, 1))
            v_meanb = em("vector", "tensor_copy", r_meanb, c_mean,
                         waits=(s_mean,), inc=(VE, 1))

            for grp in range(16):
                pair = grp % 2
                for k2 in range(2):
                    ht = grp * 2 + k2
                    w = []
                    if k2 == 0:
                        w = [g_x] if grp == 0 else []
                        if sq_grp_t[pair] is not None:
                            w.append(sq_grp_t[pair])
                    em("vector", "tensor_mul",
                       wsbb[:, (pair * 2 + k2) * 512:(pair * 2 + k2 + 1) * 512],
                       xt_tile(ci, ht), xt_tile(ci, ht),
                       waits=tuple(w), inc=(VE, 1) if k2 == 1 else None)
                v_sq = VE.at()
                for k2 in range(2):
                    ht = grp * 2 + k2
                    em("tensor", "matmul", psum2[:, :], s_ones128[:, :],
                       wsbb[:, (pair * 2 + k2) * 512:(pair * 2 + k2 + 1) * 512],
                       start=(ht == 0), stop=(ht == NHT - 1),
                       waits=(v_sq,) if k2 == 0 else (),
                       inc=(TQ, 1) if k2 == 1 else None)
                sq_grp_t[pair] = TQ.at()
            chunk_sq_done[ci] = VE.at()
            t_sumsq = sq_grp_t[15 % 2]

            for (cj, vkv_j, evd_j) in pending_vt:
                for kk in range(4):
                    em("tensor", "transpose", ps_tpb[:, kk * 128:(kk + 1) * 128],
                       sbiasb[:, kk * 128:(kk + 1) * 128], s_eyeb[:, :],
                       waits=(vkv_j, evd_j) if kk == 0 else (),
                       inc=(TQ, 1) if kk == 3 else None)
                t_vt = TQ.at()
                for kk in range(4):
                    em("vector", "tensor_copy",
                       s_kv[:, 8192 + (4 * cj + kk) * 128:
                            8192 + (4 * cj + kk + 1) * 128],
                       ps_tpb[:, kk * 128:(kk + 1) * 128],
                       waits=(t_vt,) if kk == 0 else ())
            pending_vt.clear()

            for jj, pdst in ((0, ps_a), (1, ps_b)):
                for ht in range(NHT):
                    w = []
                    if jj == 0 and ht == 0:
                        if kv_free is not None:
                            w.append(kv_free)
                        if ci == 0:
                            w.append(g_wkv)
                    em("tensor", "matmul", pdst[:, :],
                       s_wbuf[:, ht * 256 + jj * 128:ht * 256 + jj * 128 + 128],
                       xt_tile(ci, ht), start=(ht == 0), stop=False,
                       waits=tuple(w))
                em("tensor", "matmul", pdst[:, :],
                   s_negS[0:1, jj * 128:(jj + 1) * 128], r_meanb,
                   start=False, stop=True,
                   waits=(v_meanb, s_negS_done) if jj == 0 else (),
                   inc=(TQ, 1) if jj == 1 else None)
            t_kv = TQ.at()
            chunk_mm_done[ci] = t_kv

            em("scalar", "activation", c_var, psum2[0:1, :], AF.Copy,
               scale=1.0 / H, waits=(t_sumsq,))
            s_v1 = em("scalar", "activation", s_rows[0:1, 0:1],
                      s_rows[0:1, 0:1], AF.Copy, inc=(SM, 1))
            stats_free[ci % 2] = s_v1
            em("vector", "tensor_mul", c_rstd, c_mean, c_mean, waits=(s_mean,))
            v_v1 = em("vector", "tensor_sub", c_var, c_var, c_rstd,
                      waits=(s_v1,), inc=(VE, 1))
            em("scalar", "activation", c_rstd, c_var, AF.Ln,
               bias=s_sm[0:1, 3:4], waits=(v_v1, d_consts))
            s_rstd = em("scalar", "activation", c_rstd, c_rstd, AF.Exp,
                        scale=-0.5, inc=(SM, 1))
            v_rstdb = em("vector", "tensor_copy", r_rstdb, c_rstd,
                         waits=(s_rstd,), inc=(VE, 1))

            t_rb = em("tensor", "matmul", ps_s0[:, 0:512], s_onesrow[:, :],
                      r_rstdb, start=True, stop=True, waits=(v_rstdb,),
                      inc=(TQ, 1))
            v_rb = em("vector", "tensor_copy", finf[:, 0:512], ps_s0[:, 0:512],
                      waits=(t_rb,), inc=(VE, 1))

            em("vector", "tensor_tensor", s_kv[:, ci * 512:(ci + 1) * 512],
               ps_a[:, :], finf[:, 0:512], op=ALU.mult, waits=(t_kv,))
            v_kv = em("vector", "tensor_tensor", sbiasb[:, 0:512],
                      ps_b[:, :], finf[:, 0:512], op=ALU.mult, inc=(VE, 1))
            kv_free = v_kv

            pending_vt.append((ci, v_kv, None))

            for r in range(4):
                g4i = 4 * ci + r
                if g4i % 8 != 7:
                    continue
                j = g4i // 8
                v_ms = em("vector", "tensor_mul", c_var, c_mean, c_rstd,
                          inc=(VE, 1))
                t_ms = em("tensor", "matmul", ps_s0[:, 512:640],
                          s_onesrow32[:, :], c_var[0:1, r * 128:(r + 1) * 128],
                          start=True, stop=True, waits=(v_ms, d_consts),
                          inc=(TQ, 1))
                if ci == NCH - 1:
                    for ht in range(NHT):
                        xns = s_xn[:, ht * 1024 + j * 128:
                                   ht * 1024 + (j + 1) * 128]
                        w = (t_ms, d_xo) if ht == 0 else ()
                        em("vector", "tensor_tensor", xns, xns,
                           ps_s0[:, r * 128:(r + 1) * 128], op=ALU.mult,
                           waits=w)
                        em("vector", "tensor_tensor", xns, xns,
                           ps_s0[:, 512:640], op=ALU.subtract)
                    em("vector", "tensor_copy", s_sm[:, 5:6], s_sm[:, 5:6],
                       inc=(VE, 1))
                    xn_slot_done[j] = VE.at()
                else:
                    em("vector", "tensor_copy", pqf[:, j * 128:(j + 1) * 128],
                       finf[:, r * 128:(r + 1) * 128])
                    em("vector", "tensor_copy",
                       finf[:, 1024 + j * 128:1024 + (j + 1) * 128],
                       ps_s0[:, 512:640], waits=(t_ms,))
                    pending_norm.append((j, ci))
            chunk_ev_done[ci] = em("vector", "tensor_copy", s_sm[:, 5:6],
                                   s_sm[:, 5:6], inc=(VE, 1))
            row_free = chunk_ev_done[ci]
            pending_vt[-1] = (pending_vt[-1][0], pending_vt[-1][1],
                              chunk_ev_done[ci])
            keep = [p for p in pending_norm if p[1] > ci - 2]
            for j, _cc in pending_norm:
                if _cc <= ci - 2:
                    emit_norm(j)
            pending_norm[:] = keep

        for (cj, vkv_j, evd_j) in pending_vt:
            for kk in range(4):
                em("tensor", "transpose", ps_tpb[:, kk * 128:(kk + 1) * 128],
                   sbiasb[:, kk * 128:(kk + 1) * 128], s_eyeb[:, :],
                   waits=(vkv_j, evd_j, xn_slot_done[7]) if kk == 0 else (),
                   inc=(TQ, 1) if kk == 3 else None)
            t_vt = TQ.at()
            for kk in range(4):
                em("vector", "tensor_copy",
                   s_kv[:, 8192 + (4 * cj + kk) * 128:
                        8192 + (4 * cj + kk + 1) * 128],
                   ps_tpb[:, kk * 128:(kk + 1) * 128],
                   waits=(t_vt,) if kk == 0 else ())
        pending_vt.clear()
        for j, _cc in pending_norm:
            emit_norm(j)
        pending_norm.clear()
        p1_t_done = TQ.at()
        p1_v_done = VE.at()
        p1_s_done = SM.at()
        v_xn_all = p1_v_done

        v_xn = p1_v_done

        SU = [ps_s0[:, 0:512], ps_s0[:, 512:1024], ps_s1[:, 0:512]]
        ps_q = ps_s1[:, 512:1024]
        PU = [s_pq[:, u * 512:(u + 1) * 512] for u in range(3)]
        qtbuf = [s_pq[:, 1536:2560], s_fin[:, 2048:3072]]
        psq_free = [p1_s_done]
        qp_mm_done = [None] * NH
        qp_evac_done = [None] * NH

        class QF:

            def __init__(self, hq, g_wq):
                self.hq = hq
                self.g_wq = g_wq
                self.idx = 0
                self.ev0 = None

            def pop(self):
                if self.idx >= 64:
                    return False
                chunk, ht = self.idx // 32, self.idx % 32
                half = self.hq % 2
                w = []
                if self.idx == 0:
                    w = [self.g_wq, psq_free[0], v_xn]
                elif self.idx == 32:
                    w = [self.ev0]
                em("tensor", "matmul", ps_q,
                   s_wbuf[:, half * 4096 + ht * 128:half * 4096 + (ht + 1) * 128],
                   s_xn[:, ht * 1024 + chunk * 512:ht * 1024 + chunk * 512 + 512],
                   start=(ht == 0), stop=(ht == NHT - 1), waits=tuple(w),
                   inc=(TQ, 1) if ht == NHT - 1 else None)
                self.idx += 1
                if self.idx in (32, 64):
                    tq = TQ.at()
                    ve = em("vector", "tensor_copy",
                            qtbuf[self.hq % 2][:, chunk * 512:(chunk + 1) * 512],
                            ps_q, waits=(tq,), inc=(VE, 1))
                    if self.idx == 32:
                        self.ev0 = ve
                    else:
                        qp_mm_done[self.hq] = tq
                        qp_evac_done[self.hq] = ve
                        psq_free[0] = ve
                return True

        def mk_wq_dma(hq):
            half = hq % 2
            gw = [p1_t_done] if hq < 2 else [qp_mm_done[hq - 2]]
            return em("gpsimd", "dma_start",
                      out=s_wbuf[:, half * 4096:(half + 1) * 4096].rearrange(
                          "p (a d) -> p a d", d=128),
                      in_=wq[hq].rearrange("a p d -> p a d"),
                      waits=tuple(gw), inc=(G, 16))

        g_rwt = em("gpsimd", "dma_start",
                   out=s_pq[:, 0:2048].rearrange("p (a e) -> p a e", e=E),
                   in_=rwt.rearrange("a p e -> p a e"),
                   waits=(xn_slot_done[6],), inc=(G, 16))
        for chunk, pdst in ((0, ps_m0), (1, ps_m1)):
            for ht in range(NHT):
                w = (g_rwt, v_xn, p1_t_done, p1_s_done) if (chunk == 0 and ht == 0) else ()
                em("tensor", "matmul", pdst[0:64, :],
                   s_pq[:, ht * 64:(ht + 1) * 64],
                   s_xn[:, ht * 1024 + chunk * 512:ht * 1024 + chunk * 512 + 512],
                   start=(ht == 0), stop=(ht == NHT - 1), waits=w,
                   inc=(TQ, 1) if (chunk == 1 and ht == NHT - 1) else None)
        t_lg = TQ.at()
        em("vector", "tensor_scalar", finf[0:64, 0:512], ps_m0[0:64, :],
           s_rbias[:, :], None, ALU.add, waits=(t_lg, d_consts))
        s_lg = em("vector", "tensor_scalar", finf[0:64, 512:1024], ps_m1[0:64, :],
                  s_rbias[:, :], None, ALU.add, inc=(VE, 1))

        lgN = s_bias[:, 0:64]
        prb = s_bias[:, 64:128]
        mkb = s_bias[:, 128:192]
        mscN = sbiasb[:, 512:576]
        negmax = s_sm[:, 0:1]
        Zsum = s_sm[:, 4:5]
        Zr = s_sm[:, 1:2]
        wsumc = s_sm[:, 2:3]
        top8 = s_sm[:, 6:14]
        prev_v4 = None
        for ti in range(8):
            w = (s_lg,) if ti == 0 else (prev_v4,)
            t_1 = em("tensor", "transpose", ps_a[:, 0:64],
                     finf[0:64, ti * 128:(ti + 1) * 128], s_eye32[0:64, 0:64],
                     waits=w, inc=(TQ, 1))
            em("vector", "tensor_copy", lgN, ps_a[:, 0:64], waits=(t_1,))
            v_1 = em("vector", "reduce_max", negmax, lgN, axis=AX.X,
                     negate=True, inc=(VE, 1))
            s_1 = em("scalar", "activation", prb, lgN, AF.Exp, bias=negmax,
                     accum_out=Zsum, waits=(v_1,), inc=(SM, 1))
            em("vector", "max", out=top8, in_=prb, waits=(s_1,))
            em("vector", "reciprocal", Zr, Zsum)
            em("vector", "tensor_scalar", mkb, prb, top8[:, 7:8], None,
               ALU.is_ge)
            em("vector", "tensor_mul", mkb, mkb, prb)
            v_3 = em("vector", "tensor_scalar", mscN, mkb, Zr, 0.0, ALU.mult,
                     ALU.add, accum_out=wsumc, inc=(VE, 1))
            t_2a = em("tensor", "transpose", ps_bb[0:64, 0:128], mscN,
                      s_eyeb[:, :], waits=(v_3,), inc=(TQ, 1))
            t_2 = em("tensor", "transpose", ps_a[0:1, 64:192], wsumc,
                     s_eye32[:, :], inc=(TQ, 1))
            em("vector", "tensor_copy", s_msc[0:64, ti * 128:(ti + 1) * 128],
               ps_bb[0:64, 0:128], waits=(t_2,))
            prev_v4 = em("vector", "tensor_copy",
                         s_rows[0:1, ti * 128:(ti + 1) * 128],
                         ps_a[0:1, 64:192], inc=(VE, 1))
        em("tensor", "matmul", ps_s0[:, 0:512], s_onesrow32[:, :],
           s_rows[0:1, 0:512], start=True, stop=True, waits=(prev_v4,))
        t_wsb = em("tensor", "matmul", ps_s0[:, 512:1024], s_onesrow32[:, :],
                   s_rows[0:1, 512:1024], start=True, stop=True, inc=(TQ, 1))
        v_wsb = em("vector", "tensor_copy", s_wsb[:, :], ps_s0[:, :],
                   waits=(t_wsb,), inc=(VE, 1))

        unit_free = [v_wsb, v_wsb, p1_s_done]
        probs_free = [None, None, None]
        mm_free = [s_lg, s_lg]
        ab_free = [v_wsb, v_wsb]
        head_exp_done = [None] * NH
        ta_head_done = [None] * NH
        v_ev = [None] * NH
        tail_ev = [[None, None] for _ in range(NH)]

        qf = QF(0, mk_wq_dma(0))
        while qf.pop():
            pass
        next_qf = QF(1, mk_wq_dma(1))

        all_tasks = []
        meta = {}
        for (hh, slope, kts) in SCHED:
            tasks = []
            vi_base = 0
            for (kt, ja, jb, batches, diag) in kts:
                jd = (kt - 7) // 8 if diag else None
                for (a, b) in _splits(ja * 128, (jb + 1) * 128):
                    tb = []
                    for bi, (c0, c1, ref) in enumerate(batches):
                        lo_, hi_ = max(c0, a), min(c1, b)
                        if lo_ < hi_:
                            tb.append((lo_, hi_, vi_base + bi))
                    hd = jd if (jd is not None and a <= jd * 128 < b) else None
                    tasks.append((kt, a, b, tuple(tb), hd))
                vi_base += len(batches)
            assert vi_base == HEAD_NV[hh]
            lo_idx = [i for i, t in enumerate(tasks) if t[1] < 512]
            hi_idx = [i for i, t in enumerate(tasks) if t[1] >= 512]
            meta[hh] = dict(
                first_lo=min(lo_idx), last_lo=max(lo_idx),
                first_hi=min(hi_idx) if hi_idx else None,
                last_hi=max(hi_idx) if hi_idx else None,
                n=len(tasks))
            for i, t in enumerate(tasks):
                all_tasks.append((hh,) + t + (i,))

        NG = len(all_tasks)
        ts_g = [None] * NG
        se_g = [None] * NG
        vmk_g = [None] * NG
        d_bias_h = [None] * NH

        def ensure_bias(hh):
            if d_bias_h[hh] is None:
                half = hh % 2
                nv = HEAD_NV[hh]
                dw = [v_wsb] if hh < 2 else [head_exp_done[hh - 2]]
                d_bias_h[hh] = em("sync", "dma_start",
                                  out=s_bias[:, half * 160:half * 160 + nv],
                                  in_=btab[:, HEAD_OFF[hh]:HEAD_OFF[hh] + nv],
                                  waits=tuple(dw), inc=(D, 16))

        def emit_scores_g(g):
            hh, kt, a, b, tb, hd, loc = all_tasks[g]
            ensure_bias(hh)
            u = g % 3
            w = [unit_free[u]]
            if loc < 3:
                w.append(qp_evac_done[hh])
            ts_g[g] = em("tensor", "matmul", SU[u][:, 0:b - a],
                         s_kv[:, kt * 128:(kt + 1) * 128],
                         qtbuf[hh % 2][:, a:b],
                         start=True, stop=True, waits=tuple(w), inc=(TS, 1))

        def emit_exp_g(g):
            hh, kt, a, b, tb, hd, loc = all_tasks[g]
            u = g % 3
            half = hh % 2
            w = [ts_g[g], d_bias_h[hh]]
            if probs_free[u] is not None:
                w.append(probs_free[u])
            for bi, (c0, c1, vi) in enumerate(tb):
                em("scalar", "activation", PU[u][:, c0 - a:c1 - a],
                   SU[u][:, c0 - a:c1 - a], AF.Exp,
                   bias=s_bias[:, half * 160 + vi:half * 160 + vi + 1],
                   scale=SCALE,
                   waits=tuple(w) if bi == 0 else (),
                   inc=(SE, 1) if bi == len(tb) - 1 else None)
            se_g[g] = SE.at()
            unit_free[u] = se_g[g]

        def emit_mask_g(g):
            hh, kt, a, b, tb, hd, loc = all_tasks[g]
            if hd is None:
                return
            u = g % 3
            c = hd * 128 - a
            vmk_g[g] = em("vector", "tensor_mul", PU[u][:, c:c + 128],
                          PU[u][:, c:c + 128], s_tri[:, :],
                          waits=(se_g[g],), inc=(VE, 1))

        def emit_av_g(g):
            hh, kt, a, b, tb, hd, loc = all_tasks[g]
            u = g % 3
            m = meta[hh]
            lo = a < 512
            w = [se_g[g]]
            if vmk_g[g] is not None:
                w.append(vmk_g[g])
            first = (loc == m["first_lo"]) if lo else (loc == m["first_hi"])
            last = (loc == m["last_lo"]) if lo else (loc == m["last_hi"])
            if first:
                w.append(ab_free[0 if lo else 1])
                mf = mm_free[0 if lo else 1]
                if isinstance(mf, tuple) and isinstance(mf[0], tuple):
                    w.extend(mf)
                else:
                    w.append(mf)
            pd = ps_a if lo else ps_b
            pm = ps_m0 if lo else ps_m1
            em("tensor", "matmul", pd[:, a % 512:a % 512 + (b - a)],
               s_kv[:, 8192 + kt * 128:8192 + (kt + 1) * 128],
               PU[u][:, 0:b - a], start=first, stop=last, waits=tuple(w))
            em("tensor", "matmul", pm[:, a % 512:a % 512 + (b - a)],
               s_ones128[:, :], PU[u][:, 0:b - a],
               start=first, stop=last, inc=(TA, 1))
            probs_free[u] = TA.at()
            if last:
                emit_tail_bank(hh, 0 if lo else 1)

        def emit_tail_bank(hh, bank):
            ta_now = TA.at()
            pm = ps_m0 if bank == 0 else ps_m1
            pd = ps_a if bank == 0 else ps_b
            fr = finf[:, 0:512] if bank == 0 else finf[:, 512:1024]
            em("vector", "reciprocal", fr, pm[:, :], waits=(ta_now,))
            vr = em("vector", "tensor_tensor",
                    s_arena[:, hh * 1024 + bank * 512:
                            hh * 1024 + (bank + 1) * 512],
                    pd[:, :], fr, op=ALU.mult, inc=(VE, 1))
            ab_free[bank] = vr
            mm_free[bank] = vr
            tail_ev[hh][bank] = vr

        def emit_tail(hh):
            head_exp_done[hh] = SE.at()
            ta_head_done[hh] = TA.at()
            assert tail_ev[hh][0] is not None and tail_ev[hh][1] is not None
            v_ev[hh] = max(tail_ev[hh][0], tail_ev[hh][1],
                           key=lambda t: t[1])

        qf = next_qf
        next_qf = QF(2, mk_wq_dma(2))

        def lookahead(g):
            nonlocal qf, next_qf
            hh = all_tasks[g][0]
            cur = all_tasks[g - 3][0] if g >= 3 else all_tasks[0][0]
            if hh != cur and qf is not None and qf.hq == hh:
                while qf.pop():
                    pass
                qf = next_qf if next_qf is not None and next_qf.hq < NH else None
                if hh + 2 < NH:
                    next_qf = QF(hh + 2, mk_wq_dma(hh + 2))
                else:
                    next_qf = None
            emit_scores_g(g)
            emit_exp_g(g)
            emit_mask_g(g)

        for g in range(min(3, NG)):
            emit_scores_g(g)
            emit_exp_g(g)
            emit_mask_g(g)
        for g in range(NG):
            hh = all_tasks[g][0]
            loc = all_tasks[g][6]
            emit_av_g(g)
            if loc == meta[hh]["n"] - 1:
                emit_tail(hh)
            if g + 3 < NG:
                lookahead(g + 3)
            if qf is not None and qf.hq == hh + 1:
                if not qf.pop():
                    pass

        attn_done = ta_head_done[NH - 1]
        psfree = {"s0": unit_free[1], "s1": psq_free[0],
                  "a": v_ev[NH - 1], "b": v_ev[NH - 1]}
        p5_s0_extra = unit_free[0]
        p5_s1_extra = unit_free[2]
        qproj_done = qp_mm_done

        t_sT = [None] * NHT
        v_fin = [None] * NHT
        v_g2 = [None] * NHT
        g_wo_pre = [None, None]
        for jt in range(2):
            g_wo_pre[jt] = em("gpsimd", "dma_start",
                              out=s_wbuf[:, jt * 4096:(jt + 1) * 4096].rearrange(
                                  "p (a d) -> p a d", d=128),
                              in_=wo[jt].rearrange("a p d -> p a d"),
                              waits=(qproj_done[NH - 2 + jt],), inc=(G, 16))
        g_eb = em("gpsimd", "dma_start",
                  out=s_kv[0:64, 8192:12288].rearrange("p (a j) -> p a j", j=128),
                  in_=eb.rearrange("a p j -> p a j"),
                  waits=(attn_done,), inc=(G, 16))
        for jt in range(NHT):
            half = jt % 2
            if jt < 2:
                g_wo = g_wo_pre[jt]
            else:
                g_wo = em("gpsimd", "dma_start",
                          out=s_wbuf[:, half * 4096:(half + 1) * 4096].rearrange(
                              "p (a d) -> p a d", d=128),
                          in_=wo[jt].rearrange("a p d -> p a d"),
                          waits=(t_sT[jt - 2],), inc=(G, 16))
            dxw = [attn_done] if jt < 2 else [v_fin[jt - 2]]
            d_xo = em("sync", "dma_start",
                      out=kvf[:, half * 1024:(half + 1) * 1024],
                      in_=xo[jt * 128:(jt + 1) * 128, :],
                      waits=tuple(dxw), inc=(D, 16))
            psT = ps_s0 if half == 0 else ps_s1
            key = "s0" if half == 0 else "s1"
            for chunk in range(2):
                for dtt in range(NHT):
                    w = []
                    if chunk == 0 and dtt == 0:
                        w = [g_wo, psfree[key]]
                        if jt == 0:
                            w.append(v_ev[NH - 1])
                            w.append(p5_s0_extra)
                        if jt == 1:
                            w.append(p5_s1_extra)
                    em("tensor", "matmul",
                       psT[:, chunk * 512:(chunk + 1) * 512],
                       s_wbuf[:, half * 4096 + dtt * 128:half * 4096 + (dtt + 1) * 128],
                       s_arena[:, dtt * 1024 + chunk * 512:dtt * 1024 + chunk * 512 + 512],
                       start=(dtt == 0), stop=(dtt == NHT - 1), waits=tuple(w),
                       inc=(TQ, 1) if (chunk == 1 and dtt == NHT - 1) else None)
            t_sT[jt] = TQ.at()
            bw = [g_eb] if jt == 0 else [v_g2[jt - 1]]
            bw.append(psfree["a"])
            em("tensor", "matmul", ps_a[:, :],
               s_kv[0:64, 8192 + jt * 128:8192 + (jt + 1) * 128],
               s_msc[0:64, 0:512], start=True, stop=True, waits=tuple(bw))
            t_bm = em("tensor", "matmul", ps_b[:, :],
                      s_kv[0:64, 8192 + jt * 128:8192 + (jt + 1) * 128],
                      s_msc[0:64, 512:1024], start=True, stop=True,
                      inc=(TQ, 1))
            psfree["a"] = None

            g2 = finf[:, 0:1024]
            gt = finf[:, 1024:2048]
            w = [v_wsb] if jt == 0 else []
            em("vector", "tensor_tensor", g2, s_xn[:, jt * 1024:(jt + 1) * 1024],
               s_wsb[:, :], op=ALU.mult, waits=tuple(w))
            em("vector", "tensor_tensor", g2[:, 0:512], g2[:, 0:512],
               ps_a[:, :], op=ALU.add, waits=(t_bm,))
            v_g2[jt] = em("vector", "tensor_tensor", g2[:, 512:1024],
                          g2[:, 512:1024], ps_b[:, :], op=ALU.add,
                          inc=(VE, 1))
            psfree["a"] = v_g2[jt]
            s_si = em("scalar", "activation", gt, g2, AF.Silu,
                      waits=(v_g2[jt],), inc=(SM, 1))
            em("vector", "tensor_tensor", g2, psT[:, :], gt, op=ALU.mult,
               waits=(s_si, t_sT[jt]))
            v_fin[jt] = em("vector", "tensor_tensor",
                           kvf[:, 2048 + half * 1024:2048 + (half + 1) * 1024],
                           g2, kvf[:, half * 1024:(half + 1) * 1024],
                           op=ALU.add, waits=(d_xo,), inc=(VE, 1))
            psfree[key] = v_fin[jt]
            em("sync", "dma_start",
               out=outT[jt * 128:(jt + 1) * 128, :],
               in_=kvf[:, 2048 + half * 1024:2048 + (half + 1) * 1024],
               waits=(v_fin[jt],), inc=(ST, 16))
        em("sync", None, waits=(ST.at(),))

        block = en(nc.Block())

        def replay(engine, key):
            for fn, args, kw, waits, inc in PROG[key]:
                for sw in waits:
                    engine.wait_ge(sw[0], sw[1])
                if fn is None:
                    continue
                ins = getattr(engine, fn)(*args, **kw)
                if inc is not None:
                    ins.then_inc(inc[0], inc[1])

        @block.sync
        def _(engx):
            replay(engx, "sync")

        @block.gpsimd
        def _(engx):
            replay(engx, "gpsimd")

        @block.tensor
        def _(engx):
            replay(engx, "tensor")

        @block.scalar
        def _(engx):
            replay(engx, "scalar")

        @block.vector
        def _(engx):
            replay(engx, "vector")

    return nc


_NC_CACHE = None


def kernel(L0_attn_hidden_states, L0_attn_Wq, L0_attn_Wkv, L0_attn_Wo,
           L0_ffn_router_weight, L0_ffn_router_bias, L0_ffn_expert_bias):
    global _NC_CACHE
    x = np.ascontiguousarray(np.asarray(L0_attn_hidden_states, np.float32)
                             .reshape(S, H))
    Wq = np.asarray(L0_attn_Wq, np.float32)
    Wkv = np.asarray(L0_attn_Wkv, np.float32)
    Wo = np.asarray(L0_attn_Wo, np.float32)
    Rw = np.asarray(L0_ffn_router_weight, np.float32)
    Rb = np.asarray(L0_ffn_router_bias, np.float32)
    Eb = np.asarray(L0_ffn_expert_bias, np.float32)

    wq_t = np.ascontiguousarray(
        Wq.reshape(NHT, 128, NH, 128).transpose(2, 0, 1, 3))
    wo_t = np.ascontiguousarray(
        Wo.reshape(NHT, 128, NHT, 128).transpose(2, 0, 1, 3))
    wkv_t = np.ascontiguousarray(
        Wkv.reshape(NHT, 128, 2, 128).transpose(0, 2, 1, 3))
    rwt_t = np.ascontiguousarray(Rw.T.reshape(NHT, 128, E))
    eb_t = np.ascontiguousarray(Eb.reshape(E, NHT, 128).transpose(1, 0, 2))
    rb_t = np.ascontiguousarray(Rb.reshape(E, 1))
    tri = np.ascontiguousarray(
        np.tril(np.ones((128, 128), np.float32)).T)
    eye = np.eye(128, dtype=np.float32)
    onescol = np.ones((128, 128), np.float32)
    onesrow = np.ones((1, 128), np.float32)

    xT = x.T
    in_maps = []
    own_rows = []
    for c in range(NCORE):
        idx = ((8 * np.arange(8)[:, None] + c) * 128
               + np.arange(128)[None, :]).reshape(-1)
        own_rows.append(idx)
        shift = (7 - c) * 128
        xt_c = np.zeros((H, S), np.float32)
        xt_c[:, shift:] = xT[:, :S - shift]
        xo_c = np.ascontiguousarray(xT[:, idx])
        in_maps.append(dict(
            xt=xt_c, xo=xo_c, wq=wq_t, wo=wo_t, wkv=wkv_t, rwt=rwt_t,
            rbias=rb_t, eb=eb_t, btab=build_btab(c), tri=tri, eye=eye,
            onescol=onescol, onesrow=onesrow,
            epsc=np.full((1, 1), EPS, np.float32)))

    if _NC_CACHE is None:
        _NC_CACHE = build_program()
    nc = _NC_CACHE
    res = run_bass_kernel_spmd(nc, in_maps, core_ids=list(range(NCORE)))

    out = np.empty((S, H), np.float32)
    for c in range(NCORE):
        out[own_rows[c]] = res.results[c]["outT"].T
    return out.reshape(1, S, H)


# version 26
# speedup vs baseline: 1.0128x; 1.0061x over previous
import numpy as np
import ml_dtypes
import concourse.bass as bass
import concourse.mybir as mybir
from concourse.bass_utils import run_bass_kernel_spmd

F32 = mybir.dt.float32
BF16 = mybir.dt.bfloat16
AF = mybir.ActivationFunctionType
ALU = mybir.AluOpType
AX = mybir.AxisListType

S = 8192
H = 4096
NH = 32
HD = 128
E = 64
NCORE = 8
T = 1024
NHT = H // 128
SCALE = HD ** -0.5
CSAFE = 4.0
CUT = 110.0
SPAN = 56.0
EPS = 1e-5
GARBAGE_BIAS = -30000.0

SLOPES = [2.0 ** (-0.25 * (h + 1)) for h in range(NH)]


def _head_mode(h):
    s = SLOPES[h]
    if s * 128.0 > SPAN:
        return 'W64'
    for b in (8, 4, 2):
        if s * ((b - 1) * 1024 + 128) <= SPAN:
            return b
    return 1


def make_schedule():
    heads = []
    for h in range(NH):
        s = SLOPES[h]
        dmax = int((CUT / s + 127) // 128)
        mode = _head_mode(h)
        kts = []
        for kt in range(63, -1, -1):
            ja = max(0, -(-(kt - 7) // 8))
            jb = min(7, (kt + dmax - 7) // 8)
            if ja > jb:
                continue
            batches = []
            if mode == 'W64':
                for j in range(jb, ja - 1, -1):
                    for half in (0, 1):
                        c0 = j * 128 + half * 64
                        ref = (8 * j + 7) * 128 + (half + 1) * 64
                        batches.append((c0, c0 + 64, ref))
            else:
                B = mode
                for gg in reversed(range(8 // B)):
                    g0, g1 = gg * B, gg * B + B - 1
                    lo, hi = max(ja, g0), min(jb, g1)
                    if lo > hi:
                        continue
                    batches.append((lo * 128, (hi + 1) * 128,
                                    (8 * g1 + 7) * 128 + 128))
            diag = (kt % 8 == 7) and (ja <= (kt - 7) // 8 <= jb)
            kts.append((kt, ja, jb, tuple(batches), diag))
        heads.append((h, s, kts))
    return heads


SCHED = make_schedule()
HEAD_NV = [sum(len(b) for (_, _, _, b, _) in kts) for (_, _, kts) in SCHED]
HEAD_OFF = [0]
for nv in HEAD_NV:
    HEAD_OFF.append(HEAD_OFF[-1] + nv)
NVEC = HEAD_OFF[-1]
assert max(HEAD_NV) <= 160


def build_btab(core):
    tab = np.empty((128, NVEC), np.float32)
    p = np.arange(128, dtype=np.float64)
    vi = 0
    for (h, slope, kts) in SCHED:
        for (kt, ja, jb, batches, diag) in kts:
            for (c0, c1, ref) in batches:
                if kt < 7 - core:
                    tab[:, vi] = GARBAGE_BIAS
                else:
                    tab[:, vi] = (slope * (kt * 128 + p - ref) - CSAFE
                                  ).astype(np.float32)
                vi += 1
    assert vi == NVEC
    return tab


class Ctr:
    __slots__ = ("sem", "n")

    def __init__(self, sem):
        self.sem = sem
        self.n = 0

    def at(self):
        return (self.sem, self.n)


def _splits(a, b):
    out = []
    if b > 512:
        out.append((max(a, 512), b))
    if a < 512:
        out.append((a, min(b, 512)))
    return out


def build_program():
    nc = bass.Bass("TRN2", target_bir_lowering=False)

    def din(name, shape):
        return nc.dram_tensor(name, shape, F32, kind="ExternalInput").ap()

    xt = din("xt", [H, S])
    xo = din("xo", [H, T])
    wq = din("wq", [NH, NHT, 128, 128])
    wo = din("wo", [NHT, NHT, 128, 128])
    wkv = din("wkv", [NHT, 2, 128, 128])
    rwt = din("rwt", [NHT, 128, E])
    rbias = din("rbias", [E, 1])
    eb = din("eb", [NHT, E, 128])
    btab = din("btab", [128, NVEC])
    tri = din("tri", [128, 128])
    eye = din("eye", [128, 128])
    onescol = din("onescol", [128, 128])
    onesrow = din("onesrow", [1, 128])
    epsc = din("epsc", [1, 1])
    outT = nc.dram_tensor("outT", [H, T], F32, kind="ExternalOutput").ap()

    from contextlib import ExitStack
    with ExitStack() as ctx:
        en = ctx.enter_context
        s_xn = en(nc.sbuf_tensor("s_xn", [128, NHT * 1024], BF16))
        s_arena = en(nc.sbuf_tensor("s_arena", [128, NHT * 1024], BF16))
        s_kv = en(nc.sbuf_tensor("s_kv", [128, 16384], BF16))
        s_wbuf = en(nc.sbuf_tensor("s_wbuf", [128, 8192], BF16))
        s_pq = en(nc.sbuf_tensor("s_pq", [128, 3072], BF16))
        s_fin = en(nc.sbuf_tensor("s_fin", [128, 4096], BF16))
        s_bias = en(nc.sbuf_tensor("s_bias", [128, 320], F32))
        s_msc = en(nc.sbuf_tensor("s_msc", [64, 1024], BF16))
        s_wsb = en(nc.sbuf_tensor("s_wsb", [128, 1024], F32))
        s_rows = en(nc.sbuf_tensor("s_rows", [1, 1536], F32))
        s_rows_bf = en(nc.sbuf_tensor("s_rows_bf", [1, 1024], BF16))
        s_negS = en(nc.sbuf_tensor("s_negS", [1, 256], BF16))
        s_tri = en(nc.sbuf_tensor("s_tri", [128, 128], BF16))
        s_eyeb = en(nc.sbuf_tensor("s_eyeb", [128, 128], BF16))
        s_eye32 = en(nc.sbuf_tensor("s_eye32", [128, 128], F32))
        s_ones128 = en(nc.sbuf_tensor("s_ones128", [128, 128], BF16))
        s_onesrow = en(nc.sbuf_tensor("s_onesrow", [1, 128], BF16))
        s_onesrow32 = en(nc.sbuf_tensor("s_onesrow32", [1, 128], F32))
        s_rbias = en(nc.sbuf_tensor("s_rbias", [E, 1], F32))
        s_sm = en(nc.sbuf_tensor("s_sm", [128, 14], F32))

        ps_s0 = en(nc.psum_tensor("ps_s0", [128, 1024], F32))
        ps_s1 = en(nc.psum_tensor("ps_s1", [128, 1024], F32))
        ps_a = en(nc.psum_tensor("ps_a", [128, 512], F32))
        ps_b = en(nc.psum_tensor("ps_b", [128, 512], F32))
        ps_m0 = en(nc.psum_tensor("ps_m0", [128, 512], F32))
        ps_m1 = en(nc.psum_tensor("ps_m1", [128, 512], F32))

        sems = {}
        for nm in ("g", "d", "st", "tS", "tA", "tQ", "sE", "sM", "vE"):
            sems[nm] = en(nc.semaphore("sem_" + nm))
        G = Ctr(sems["g"]); D = Ctr(sems["d"]); ST = Ctr(sems["st"])
        TS = Ctr(sems["tS"]); TA = Ctr(sems["tA"]); TQ = Ctr(sems["tQ"])
        SE = Ctr(sems["sE"]); SM = Ctr(sems["sM"]); VE = Ctr(sems["vE"])

        PROG = {"sync": [], "gpsimd": [], "tensor": [], "scalar": [], "vector": []}

        def em(eng, fn, *args, waits=(), inc=None, **kw):
            rec_inc = None
            ret = None
            if inc is not None:
                ctr, amt = inc
                ctr.n += amt
                rec_inc = (ctr.sem, amt)
                ret = (ctr.sem, ctr.n)
            PROG[eng].append((fn, args, kw, tuple(w for w in waits if w), rec_inc))
            return ret

        pqf = s_pq[:, :].bitcast(F32)
        finf = s_fin[:, :].bitcast(F32)
        kvf = s_kv[:, :].bitcast(F32)
        sbiasb = s_bias[:, :].bitcast(BF16)
        wsbb = s_wsb[:, :].bitcast(BF16)
        ps_s1b = ps_s1[:, :].bitcast(BF16)
        ps_tpb = ps_s0[:, 640:896].bitcast(BF16)
        ps_bb = ps_b[:, :].bitcast(BF16)

        em("gpsimd", "dma_start", out=s_eyeb[:, :], in_=eye[:, :], inc=(G, 16))
        em("gpsimd", "dma_start", out=s_tri[:, :], in_=tri[:, :], inc=(G, 16))
        em("gpsimd", "dma_start", out=s_ones128[:, :], in_=onescol[:, :], inc=(G, 16))
        em("gpsimd", "dma_start", out=s_onesrow[:, :], in_=onesrow[:, :], inc=(G, 16))
        g_wkv = em("gpsimd", "dma_start",
                   out=s_wbuf.rearrange("p (a b j) -> p a b j", b=2, j=128),
                   in_=wkv.rearrange("a b p j -> p a b j"), inc=(G, 16))
        em("sync", "dma_start", out=s_eye32[:, :], in_=eye[:, :], inc=(D, 16))
        em("sync", "dma_start", out=s_onesrow32[:, :], in_=onesrow[:, :], inc=(D, 16))
        em("sync", "dma_start", out=s_rbias[:, :], in_=rbias[:, :], inc=(D, 16))
        d_consts = em("sync", "dma_start", out=s_sm[0:1, 3:4], in_=epsc[:, :],
                      inc=(D, 16))

        for jj in range(2):
            for ht in range(NHT):
                w = (g_wkv,) if (jj == 0 and ht == 0) else ()
                last = (jj == 1 and ht == NHT - 1)
                em("tensor", "matmul", ps_a[:, jj:jj + 1],
                   s_wbuf[:, ht * 256 + jj * 128:ht * 256 + jj * 128 + 128],
                   s_ones128[:, 0:1], start=(ht == 0), stop=(ht == NHT - 1),
                   waits=w, inc=(TQ, 1) if last else None)
        t_cs = TQ.at()
        v_cs = em("vector", "tensor_copy", sbiasb[:, 0:2], ps_a[:, 0:2],
                  waits=(t_cs,), inc=(VE, 1))
        for jj in range(2):
            em("tensor", "transpose", ps_bb[0:1, jj * 128:(jj + 1) * 128],
               sbiasb[:, jj:jj + 1], s_eyeb[:, :],
               waits=(v_cs,) if jj == 0 else (),
               inc=(TQ, 1) if jj == 1 else None)
        t_tp = TQ.at()
        for jj in range(2):
            em("scalar", "activation", s_negS[0:1, jj * 128:(jj + 1) * 128],
               ps_bb[0:1, jj * 128:(jj + 1) * 128], AF.Copy, scale=-1.0,
               waits=(t_tp,) if jj == 0 else (),
               inc=(SM, 1) if jj == 1 else None)
        s_negS_done = SM.at()

        NCH = 16
        c_mean = s_rows[0:1, 0:512]
        c_var = s_rows[0:1, 512:1024]
        c_rstd = s_rows[0:1, 1024:1536]
        r_meanb = s_rows_bf[0:1, 0:512]
        r_rstdb = s_rows_bf[0:1, 512:1024]
        chunk_mm_done = [None] * NCH
        chunk_sq_done = [None] * NCH
        chunk_ev_done = [None] * NCH

        def xt_tile(ci, ht):
            base = (ci % 2) * 16384 + (ht // 8) * 4096 + (ht % 8) * 512
            return s_arena[:, base:base + 512]

        xn_slot_done = [None] * 8
        pending_norm = []
        d_xo = None
        last_grp_t = None
        sq_grp_t = [None, None]
        stats_free = [None, None]
        kv_free = None
        pending_vt = []
        row_free = None

        def emit_norm(j):
            for ht in range(NHT):
                xns = s_xn[:, ht * 1024 + j * 128:ht * 1024 + (j + 1) * 128]
                w = (d_xo,) if ht == 0 else ()
                em("vector", "tensor_tensor", xns, xns,
                   pqf[:, j * 128:(j + 1) * 128], op=ALU.mult, waits=w)
                em("vector", "tensor_tensor", xns, xns,
                   finf[:, 1024 + j * 128:1024 + (j + 1) * 128],
                   op=ALU.subtract)
            em("vector", "tensor_copy", s_sm[:, 5:6], s_sm[:, 5:6],
               inc=(VE, 1))
            xn_slot_done[j] = VE.at()

        for ci in range(NCH):
            w0 = []
            if ci >= 2:
                w0 = [chunk_mm_done[ci - 2], chunk_sq_done[ci - 2]]
            for g4 in range(4):
                src = xt[g4 * 1024:(g4 + 1) * 1024, ci * 512:(ci + 1) * 512]
                src = src.rearrange("(ht p) t -> p ht t", p=128)
                dst = s_arena[:, (ci % 2) * 16384 + g4 * 4096:
                              (ci % 2) * 16384 + (g4 + 1) * 4096]
                dst = dst.rearrange("p (ht t) -> p ht t", t=512)
                em("gpsimd", "dma_start", out=dst, in_=src,
                   waits=tuple(w0) if g4 == 0 else (), inc=(G, 16))
            g_x = G.at()
            if ci == 1:
                for gg in range(2):
                    src = xo[gg * 2048:(gg + 1) * 2048, :].rearrange(
                        "(ht p) t -> p ht t", p=128)
                    dst = s_xn[:, gg * 16384:(gg + 1) * 16384].rearrange(
                        "p (ht t) -> p ht t", t=1024)
                    em("gpsimd", "dma_start", out=dst, in_=src, inc=(G, 16))
                d_xo = G.at()

            pmean = ps_m0 if ci % 2 == 0 else ps_s1[:, 0:512]
            psum2 = ps_m1 if ci % 2 == 0 else ps_s1[:, 512:1024]
            tw = [g_x]
            if stats_free[ci % 2] is not None:
                tw.append(stats_free[ci % 2])
            for ht in range(NHT):
                em("tensor", "matmul", pmean[:, :], s_ones128[:, :],
                   xt_tile(ci, ht), start=(ht == 0), stop=(ht == NHT - 1),
                   waits=tuple(tw) if ht == 0 else (),
                   inc=(TQ, 1) if ht == NHT - 1 else None)
            t_mean = TQ.at()

            mw = [t_mean]
            if row_free is not None:
                mw.append(row_free)
            s_mean = em("scalar", "activation", c_mean, pmean[0:1, :], AF.Copy,
                        scale=1.0 / H, waits=tuple(mw), inc=(SM, 1))
            v_meanb = em("vector", "tensor_copy", r_meanb, c_mean,
                         waits=(s_mean,), inc=(VE, 1))

            for grp in range(16):
                pair = grp % 2
                for k2 in range(2):
                    ht = grp * 2 + k2
                    w = []
                    if k2 == 0:
                        w = [g_x] if grp == 0 else []
                        if sq_grp_t[pair] is not None:
                            w.append(sq_grp_t[pair])
                    em("vector", "tensor_mul",
                       wsbb[:, (pair * 2 + k2) * 512:(pair * 2 + k2 + 1) * 512],
                       xt_tile(ci, ht), xt_tile(ci, ht),
                       waits=tuple(w), inc=(VE, 1) if k2 == 1 else None)
                v_sq = VE.at()
                for k2 in range(2):
                    ht = grp * 2 + k2
                    em("tensor", "matmul", psum2[:, :], s_ones128[:, :],
                       wsbb[:, (pair * 2 + k2) * 512:(pair * 2 + k2 + 1) * 512],
                       start=(ht == 0), stop=(ht == NHT - 1),
                       waits=(v_sq,) if k2 == 0 else (),
                       inc=(TQ, 1) if k2 == 1 else None)
                sq_grp_t[pair] = TQ.at()
            chunk_sq_done[ci] = VE.at()
            t_sumsq = sq_grp_t[15 % 2]

            for (cj, vkv_j, evd_j) in pending_vt:
                for kk in range(4):
                    em("tensor", "transpose", ps_tpb[:, kk * 128:(kk + 1) * 128],
                       sbiasb[:, kk * 128:(kk + 1) * 128], s_eyeb[:, :],
                       waits=(vkv_j, evd_j) if kk == 0 else (),
                       inc=(TQ, 1) if kk == 3 else None)
                t_vt = TQ.at()
                for kk in range(4):
                    em("vector", "tensor_copy",
                       s_kv[:, 8192 + (4 * cj + kk) * 128:
                            8192 + (4 * cj + kk + 1) * 128],
                       ps_tpb[:, kk * 128:(kk + 1) * 128],
                       waits=(t_vt,) if kk == 0 else ())
            pending_vt.clear()

            for jj, pdst in ((0, ps_a), (1, ps_b)):
                for ht in range(NHT):
                    w = []
                    if jj == 0 and ht == 0:
                        if kv_free is not None:
                            w.append(kv_free)
                        if ci == 0:
                            w.append(g_wkv)
                    em("tensor", "matmul", pdst[:, :],
                       s_wbuf[:, ht * 256 + jj * 128:ht * 256 + jj * 128 + 128],
                       xt_tile(ci, ht), start=(ht == 0), stop=False,
                       waits=tuple(w))
                em("tensor", "matmul", pdst[:, :],
                   s_negS[0:1, jj * 128:(jj + 1) * 128], r_meanb,
                   start=False, stop=True,
                   waits=(v_meanb, s_negS_done) if jj == 0 else (),
                   inc=(TQ, 1) if jj == 1 else None)
            t_kv = TQ.at()
            chunk_mm_done[ci] = t_kv

            em("scalar", "activation", c_var, psum2[0:1, :], AF.Copy,
               scale=1.0 / H, waits=(t_sumsq,))
            s_v1 = em("scalar", "activation", s_rows[0:1, 0:1],
                      s_rows[0:1, 0:1], AF.Copy, inc=(SM, 1))
            stats_free[ci % 2] = s_v1
            em("vector", "tensor_mul", c_rstd, c_mean, c_mean, waits=(s_mean,))
            v_v1 = em("vector", "tensor_sub", c_var, c_var, c_rstd,
                      waits=(s_v1,), inc=(VE, 1))
            em("scalar", "activation", c_rstd, c_var, AF.Ln,
               bias=s_sm[0:1, 3:4], waits=(v_v1, d_consts))
            s_rstd = em("scalar", "activation", c_rstd, c_rstd, AF.Exp,
                        scale=-0.5, inc=(SM, 1))
            v_rstdb = em("vector", "tensor_copy", r_rstdb, c_rstd,
                         waits=(s_rstd,), inc=(VE, 1))

            t_rb = em("tensor", "matmul", ps_s0[:, 0:512], s_onesrow[:, :],
                      r_rstdb, start=True, stop=True, waits=(v_rstdb,),
                      inc=(TQ, 1))
            v_rb = em("vector", "tensor_copy", finf[:, 0:512], ps_s0[:, 0:512],
                      waits=(t_rb,), inc=(VE, 1))

            em("vector", "tensor_tensor", s_kv[:, ci * 512:(ci + 1) * 512],
               ps_a[:, :], finf[:, 0:512], op=ALU.mult, waits=(t_kv,))
            v_kv = em("vector", "tensor_tensor", sbiasb[:, 0:512],
                      ps_b[:, :], finf[:, 0:512], op=ALU.mult, inc=(VE, 1))
            kv_free = v_kv

            pending_vt.append((ci, v_kv, None))

            for r in range(4):
                g4i = 4 * ci + r
                if g4i % 8 != 7:
                    continue
                j = g4i // 8
                v_ms = em("vector", "tensor_mul", c_var, c_mean, c_rstd,
                          inc=(VE, 1))
                t_ms = em("tensor", "matmul", ps_s0[:, 512:640],
                          s_onesrow32[:, :], c_var[0:1, r * 128:(r + 1) * 128],
                          start=True, stop=True, waits=(v_ms, d_consts),
                          inc=(TQ, 1))
                if ci == NCH - 1:
                    for ht in range(NHT):
                        xns = s_xn[:, ht * 1024 + j * 128:
                                   ht * 1024 + (j + 1) * 128]
                        w = (t_ms, d_xo) if ht == 0 else ()
                        em("vector", "tensor_tensor", xns, xns,
                           ps_s0[:, r * 128:(r + 1) * 128], op=ALU.mult,
                           waits=w)
                        em("vector", "tensor_tensor", xns, xns,
                           ps_s0[:, 512:640], op=ALU.subtract)
                    em("vector", "tensor_copy", s_sm[:, 5:6], s_sm[:, 5:6],
                       inc=(VE, 1))
                    xn_slot_done[j] = VE.at()
                else:
                    em("vector", "tensor_copy", pqf[:, j * 128:(j + 1) * 128],
                       finf[:, r * 128:(r + 1) * 128])
                    em("vector", "tensor_copy",
                       finf[:, 1024 + j * 128:1024 + (j + 1) * 128],
                       ps_s0[:, 512:640], waits=(t_ms,))
                    pending_norm.append((j, ci))
            chunk_ev_done[ci] = em("vector", "tensor_copy", s_sm[:, 5:6],
                                   s_sm[:, 5:6], inc=(VE, 1))
            row_free = chunk_ev_done[ci]
            pending_vt[-1] = (pending_vt[-1][0], pending_vt[-1][1],
                              chunk_ev_done[ci])
            keep = [p for p in pending_norm if p[1] > ci - 2]
            for j, _cc in pending_norm:
                if _cc <= ci - 2:
                    emit_norm(j)
            pending_norm[:] = keep

        for (cj, vkv_j, evd_j) in pending_vt:
            for kk in range(4):
                em("tensor", "transpose", ps_tpb[:, kk * 128:(kk + 1) * 128],
                   sbiasb[:, kk * 128:(kk + 1) * 128], s_eyeb[:, :],
                   waits=(vkv_j, evd_j, xn_slot_done[7]) if kk == 0 else (),
                   inc=(TQ, 1) if kk == 3 else None)
            t_vt = TQ.at()
            for kk in range(4):
                em("vector", "tensor_copy",
                   s_kv[:, 8192 + (4 * cj + kk) * 128:
                        8192 + (4 * cj + kk + 1) * 128],
                   ps_tpb[:, kk * 128:(kk + 1) * 128],
                   waits=(t_vt,) if kk == 0 else ())
        pending_vt.clear()
        for j, _cc in pending_norm:
            emit_norm(j)
        pending_norm.clear()
        p1_t_done = TQ.at()
        p1_v_done = VE.at()
        p1_s_done = SM.at()
        v_xn_all = p1_v_done

        v_xn = p1_v_done

        SU = [ps_s0[:, 0:512], ps_s0[:, 512:1024], ps_s1[:, 0:512]]
        ps_q = ps_s1[:, 512:1024]
        PU = [s_pq[:, u * 512:(u + 1) * 512] for u in range(3)]
        qtbuf = [s_pq[:, 1536:2560], s_fin[:, 2048:3072],
                 s_fin[:, 3072:4096]]
        psq_free = [p1_s_done]
        qp_mm_done = [None] * NH
        qp_evac_done = [None] * NH

        class QF:

            def __init__(self, hq, g_wq):
                self.hq = hq
                self.g_wq = g_wq
                self.idx = 0
                self.ev0 = None

            def pop(self):
                if self.idx >= 64:
                    return False
                chunk, ht = self.idx // 32, self.idx % 32
                half = self.hq % 2
                w = []
                if self.idx == 0:
                    w = [self.g_wq, psq_free[0], v_xn]
                elif self.idx == 32:
                    w = [self.ev0]
                em("tensor", "matmul", ps_q,
                   s_wbuf[:, half * 4096 + ht * 128:half * 4096 + (ht + 1) * 128],
                   s_xn[:, ht * 1024 + chunk * 512:ht * 1024 + chunk * 512 + 512],
                   start=(ht == 0), stop=(ht == NHT - 1), waits=tuple(w),
                   inc=(TQ, 1) if ht == NHT - 1 else None)
                self.idx += 1
                if self.idx in (32, 64):
                    tq = TQ.at()
                    ve = em("vector", "tensor_copy",
                            qtbuf[self.hq % 3][:, chunk * 512:(chunk + 1) * 512],
                            ps_q, waits=(tq,), inc=(VE, 1))
                    if self.idx == 32:
                        self.ev0 = ve
                    else:
                        qp_mm_done[self.hq] = tq
                        qp_evac_done[self.hq] = ve
                        psq_free[0] = ve
                return True

        def mk_wq_dma(hq):
            half = hq % 2
            gw = [p1_t_done] if hq < 2 else [qp_mm_done[hq - 2]]
            return em("gpsimd", "dma_start",
                      out=s_wbuf[:, half * 4096:(half + 1) * 4096].rearrange(
                          "p (a d) -> p a d", d=128),
                      in_=wq[hq].rearrange("a p d -> p a d"),
                      waits=tuple(gw), inc=(G, 16))

        g_rwt = em("gpsimd", "dma_start",
                   out=s_pq[:, 0:2048].rearrange("p (a e) -> p a e", e=E),
                   in_=rwt.rearrange("a p e -> p a e"),
                   waits=(xn_slot_done[6],), inc=(G, 16))
        for chunk, pdst in ((0, ps_m0), (1, ps_m1)):
            for ht in range(NHT):
                w = (g_rwt, v_xn, p1_t_done, p1_s_done) if (chunk == 0 and ht == 0) else ()
                em("tensor", "matmul", pdst[0:64, :],
                   s_pq[:, ht * 64:(ht + 1) * 64],
                   s_xn[:, ht * 1024 + chunk * 512:ht * 1024 + chunk * 512 + 512],
                   start=(ht == 0), stop=(ht == NHT - 1), waits=w,
                   inc=(TQ, 1) if (chunk == 1 and ht == NHT - 1) else None)
        t_lg = TQ.at()
        em("vector", "tensor_scalar", finf[0:64, 0:512], ps_m0[0:64, :],
           s_rbias[:, :], None, ALU.add, waits=(t_lg, d_consts))
        s_lg = em("vector", "tensor_scalar", finf[0:64, 512:1024], ps_m1[0:64, :],
                  s_rbias[:, :], None, ALU.add, inc=(VE, 1))

        lgN = s_bias[:, 0:64]
        prb = s_bias[:, 64:128]
        mkb = s_bias[:, 128:192]
        mscN = sbiasb[:, 512:576]
        negmax = s_sm[:, 0:1]
        Zsum = s_sm[:, 4:5]
        Zr = s_sm[:, 1:2]
        wsumc = s_sm[:, 2:3]
        top8 = s_sm[:, 6:14]
        prev_v4 = None
        for ti in range(8):
            w = (s_lg,) if ti == 0 else (prev_v4,)
            t_1 = em("tensor", "transpose", ps_a[:, 0:64],
                     finf[0:64, ti * 128:(ti + 1) * 128], s_eye32[0:64, 0:64],
                     waits=w, inc=(TQ, 1))
            em("vector", "tensor_copy", lgN, ps_a[:, 0:64], waits=(t_1,))
            v_1 = em("vector", "reduce_max", negmax, lgN, axis=AX.X,
                     negate=True, inc=(VE, 1))
            s_1 = em("scalar", "activation", prb, lgN, AF.Exp, bias=negmax,
                     accum_out=Zsum, waits=(v_1,), inc=(SM, 1))
            em("vector", "max", out=top8, in_=prb, waits=(s_1,))
            em("vector", "reciprocal", Zr, Zsum)
            em("vector", "tensor_scalar", mkb, prb, top8[:, 7:8], None,
               ALU.is_ge)
            em("vector", "tensor_mul", mkb, mkb, prb)
            v_3 = em("vector", "tensor_scalar", mscN, mkb, Zr, 0.0, ALU.mult,
                     ALU.add, accum_out=wsumc, inc=(VE, 1))
            t_2a = em("tensor", "transpose", ps_bb[0:64, 0:128], mscN,
                      s_eyeb[:, :], waits=(v_3,), inc=(TQ, 1))
            t_2 = em("tensor", "transpose", ps_a[0:1, 64:192], wsumc,
                     s_eye32[:, :], inc=(TQ, 1))
            em("vector", "tensor_copy", s_msc[0:64, ti * 128:(ti + 1) * 128],
               ps_bb[0:64, 0:128], waits=(t_2,))
            prev_v4 = em("vector", "tensor_copy",
                         s_rows[0:1, ti * 128:(ti + 1) * 128],
                         ps_a[0:1, 64:192], inc=(VE, 1))
        em("tensor", "matmul", ps_s0[:, 0:512], s_onesrow32[:, :],
           s_rows[0:1, 0:512], start=True, stop=True, waits=(prev_v4,))
        t_wsb = em("tensor", "matmul", ps_s0[:, 512:1024], s_onesrow32[:, :],
                   s_rows[0:1, 512:1024], start=True, stop=True, inc=(TQ, 1))
        v_wsb = em("vector", "tensor_copy", s_wsb[:, :], ps_s0[:, :],
                   waits=(t_wsb,), inc=(VE, 1))

        unit_free = [v_wsb, v_wsb, p1_s_done]
        probs_free = [None, None, None]
        mm_free = [s_lg, s_lg]
        ab_free = [v_wsb, v_wsb]
        head_exp_done = [None] * NH
        ta_head_done = [None] * NH
        v_ev = [None] * NH
        tail_ev = [[None, None] for _ in range(NH)]

        qf = QF(0, mk_wq_dma(0))
        while qf.pop():
            pass
        next_qf = QF(1, mk_wq_dma(1))

        all_tasks = []
        meta = {}
        for (hh, slope, kts) in SCHED:
            tasks = []
            vi_base = 0
            for (kt, ja, jb, batches, diag) in kts:
                jd = (kt - 7) // 8 if diag else None
                for (a, b) in _splits(ja * 128, (jb + 1) * 128):
                    tb = []
                    for bi, (c0, c1, ref) in enumerate(batches):
                        lo_, hi_ = max(c0, a), min(c1, b)
                        if lo_ < hi_:
                            tb.append((lo_, hi_, vi_base + bi))
                    hd = jd if (jd is not None and a <= jd * 128 < b) else None
                    tasks.append((kt, a, b, tuple(tb), hd))
                vi_base += len(batches)
            assert vi_base == HEAD_NV[hh]
            lo_idx = [i for i, t in enumerate(tasks) if t[1] < 512]
            hi_idx = [i for i, t in enumerate(tasks) if t[1] >= 512]
            meta[hh] = dict(
                first_lo=min(lo_idx), last_lo=max(lo_idx),
                first_hi=min(hi_idx) if hi_idx else None,
                last_hi=max(hi_idx) if hi_idx else None,
                n=len(tasks))
            for i, t in enumerate(tasks):
                all_tasks.append((hh,) + t + (i,))

        NG = len(all_tasks)
        ts_g = [None] * NG
        se_g = [None] * NG
        vmk_g = [None] * NG
        d_bias_h = [None] * NH

        def ensure_bias(hh):
            if d_bias_h[hh] is None:
                half = hh % 2
                nv = HEAD_NV[hh]
                dw = [v_wsb] if hh < 2 else [head_exp_done[hh - 2]]
                d_bias_h[hh] = em("sync", "dma_start",
                                  out=s_bias[:, half * 160:half * 160 + nv],
                                  in_=btab[:, HEAD_OFF[hh]:HEAD_OFF[hh] + nv],
                                  waits=tuple(dw), inc=(D, 16))

        def emit_scores_g(g):
            hh, kt, a, b, tb, hd, loc = all_tasks[g]
            ensure_bias(hh)
            u = g % 3
            w = [unit_free[u]]
            if loc < 3:
                w.append(qp_evac_done[hh])
            ts_g[g] = em("tensor", "matmul", SU[u][:, 0:b - a],
                         s_kv[:, kt * 128:(kt + 1) * 128],
                         qtbuf[hh % 3][:, a:b],
                         start=True, stop=True, waits=tuple(w), inc=(TS, 1))

        def emit_exp_g(g):
            hh, kt, a, b, tb, hd, loc = all_tasks[g]
            u = g % 3
            half = hh % 2
            w = [ts_g[g], d_bias_h[hh]]
            if probs_free[u] is not None:
                w.append(probs_free[u])
            for bi, (c0, c1, vi) in enumerate(tb):
                em("scalar", "activation", PU[u][:, c0 - a:c1 - a],
                   SU[u][:, c0 - a:c1 - a], AF.Exp,
                   bias=s_bias[:, half * 160 + vi:half * 160 + vi + 1],
                   scale=SCALE,
                   waits=tuple(w) if bi == 0 else (),
                   inc=(SE, 1) if bi == len(tb) - 1 else None)
            se_g[g] = SE.at()
            unit_free[u] = se_g[g]

        def emit_mask_g(g):
            hh, kt, a, b, tb, hd, loc = all_tasks[g]
            if hd is None:
                return
            u = g % 3
            c = hd * 128 - a
            vmk_g[g] = em("vector", "tensor_mul", PU[u][:, c:c + 128],
                          PU[u][:, c:c + 128], s_tri[:, :],
                          waits=(se_g[g],), inc=(VE, 1))

        def emit_av_g(g):
            hh, kt, a, b, tb, hd, loc = all_tasks[g]
            u = g % 3
            m = meta[hh]
            lo = a < 512
            w = [se_g[g]]
            if vmk_g[g] is not None:
                w.append(vmk_g[g])
            first = (loc == m["first_lo"]) if lo else (loc == m["first_hi"])
            last = (loc == m["last_lo"]) if lo else (loc == m["last_hi"])
            if first:
                w.append(ab_free[0 if lo else 1])
                mf = mm_free[0 if lo else 1]
                if isinstance(mf, tuple) and isinstance(mf[0], tuple):
                    w.extend(mf)
                else:
                    w.append(mf)
            pd = ps_a if lo else ps_b
            pm = ps_m0 if lo else ps_m1
            em("tensor", "matmul", pd[:, a % 512:a % 512 + (b - a)],
               s_kv[:, 8192 + kt * 128:8192 + (kt + 1) * 128],
               PU[u][:, 0:b - a], start=first, stop=last, waits=tuple(w))
            em("tensor", "matmul", pm[:, a % 512:a % 512 + (b - a)],
               s_ones128[:, :], PU[u][:, 0:b - a],
               start=first, stop=last, inc=(TA, 1))
            probs_free[u] = TA.at()
            if last:
                emit_tail_bank(hh, 0 if lo else 1)

        def emit_tail_bank(hh, bank):
            ta_now = TA.at()
            pm = ps_m0 if bank == 0 else ps_m1
            pd = ps_a if bank == 0 else ps_b
            fr = finf[:, 0:512] if bank == 0 else finf[:, 512:1024]
            em("vector", "reciprocal", fr, pm[:, :], waits=(ta_now,))
            vr = em("vector", "tensor_tensor",
                    s_arena[:, hh * 1024 + bank * 512:
                            hh * 1024 + (bank + 1) * 512],
                    pd[:, :], fr, op=ALU.mult, inc=(VE, 1))
            ab_free[bank] = vr
            mm_free[bank] = vr
            tail_ev[hh][bank] = vr

        def emit_tail(hh):
            head_exp_done[hh] = SE.at()
            ta_head_done[hh] = TA.at()
            assert tail_ev[hh][0] is not None and tail_ev[hh][1] is not None
            v_ev[hh] = max(tail_ev[hh][0], tail_ev[hh][1],
                           key=lambda t: t[1])

        qf = next_qf
        next_qf = QF(2, mk_wq_dma(2))

        def lookahead(g):
            nonlocal qf, next_qf
            hh = all_tasks[g][0]
            cur = all_tasks[g - 3][0] if g >= 3 else all_tasks[0][0]
            if hh != cur and qf is not None and qf.hq == hh:
                while qf.pop():
                    pass
                qf = next_qf if next_qf is not None and next_qf.hq < NH else None
                if qf is not None and hh + 2 < NH and qf.hq == hh + 1:
                    next_qf = QF(hh + 2, mk_wq_dma(hh + 2))
                else:
                    next_qf = None
            emit_scores_g(g)
            emit_exp_g(g)
            emit_mask_g(g)

        for g in range(min(3, NG)):
            emit_scores_g(g)
            emit_exp_g(g)
            emit_mask_g(g)
        for g in range(NG):
            hh = all_tasks[g][0]
            loc = all_tasks[g][6]
            emit_av_g(g)
            if loc == meta[hh]["n"] - 1:
                emit_tail(hh)
            if g + 3 < NG:
                lookahead(g + 3)
            if qf is not None and qf.hq in (hh + 1, hh + 2):
                if not qf.pop() and qf.hq == hh + 1 and next_qf is not None:
                    qf = next_qf
                    if hh + 3 < NH:
                        next_qf = QF(hh + 3, mk_wq_dma(hh + 3))
                    else:
                        next_qf = None
                    qf.pop()

        attn_done = ta_head_done[NH - 1]
        psfree = {"s0": unit_free[1], "s1": psq_free[0],
                  "a": v_ev[NH - 1], "b": v_ev[NH - 1]}
        p5_s0_extra = unit_free[0]
        p5_s1_extra = unit_free[2]
        qproj_done = qp_mm_done

        t_sT = [None] * NHT
        v_fin = [None] * NHT
        v_g2 = [None] * NHT
        g_wo_pre = [None, None]
        for jt in range(2):
            g_wo_pre[jt] = em("gpsimd", "dma_start",
                              out=s_wbuf[:, jt * 4096:(jt + 1) * 4096].rearrange(
                                  "p (a d) -> p a d", d=128),
                              in_=wo[jt].rearrange("a p d -> p a d"),
                              waits=(qproj_done[NH - 2 + jt],), inc=(G, 16))
        g_eb = em("gpsimd", "dma_start",
                  out=s_kv[0:64, 8192:12288].rearrange("p (a j) -> p a j", j=128),
                  in_=eb.rearrange("a p j -> p a j"),
                  waits=(attn_done,), inc=(G, 16))
        for jt in range(NHT):
            half = jt % 2
            if jt < 2:
                g_wo = g_wo_pre[jt]
            else:
                g_wo = em("gpsimd", "dma_start",
                          out=s_wbuf[:, half * 4096:(half + 1) * 4096].rearrange(
                              "p (a d) -> p a d", d=128),
                          in_=wo[jt].rearrange("a p d -> p a d"),
                          waits=(t_sT[jt - 2],), inc=(G, 16))
            dxw = [attn_done] if jt < 2 else [v_fin[jt - 2]]
            d_xo = em("sync", "dma_start",
                      out=kvf[:, half * 1024:(half + 1) * 1024],
                      in_=xo[jt * 128:(jt + 1) * 128, :],
                      waits=tuple(dxw), inc=(D, 16))
            psT = ps_s0 if half == 0 else ps_s1
            key = "s0" if half == 0 else "s1"
            for chunk in range(2):
                for dtt in range(NHT):
                    w = []
                    if chunk == 0 and dtt == 0:
                        w = [g_wo, psfree[key]]
                        if jt == 0:
                            w.append(v_ev[NH - 1])
                            w.append(p5_s0_extra)
                        if jt == 1:
                            w.append(p5_s1_extra)
                    em("tensor", "matmul",
                       psT[:, chunk * 512:(chunk + 1) * 512],
                       s_wbuf[:, half * 4096 + dtt * 128:half * 4096 + (dtt + 1) * 128],
                       s_arena[:, dtt * 1024 + chunk * 512:dtt * 1024 + chunk * 512 + 512],
                       start=(dtt == 0), stop=(dtt == NHT - 1), waits=tuple(w),
                       inc=(TQ, 1) if (chunk == 1 and dtt == NHT - 1) else None)
            t_sT[jt] = TQ.at()
            bw = [g_eb] if jt == 0 else [v_g2[jt - 1]]
            bw.append(psfree["a"])
            em("tensor", "matmul", ps_a[:, :],
               s_kv[0:64, 8192 + jt * 128:8192 + (jt + 1) * 128],
               s_msc[0:64, 0:512], start=True, stop=True, waits=tuple(bw))
            t_bm = em("tensor", "matmul", ps_b[:, :],
                      s_kv[0:64, 8192 + jt * 128:8192 + (jt + 1) * 128],
                      s_msc[0:64, 512:1024], start=True, stop=True,
                      inc=(TQ, 1))
            psfree["a"] = None

            g2 = finf[:, 0:1024]
            gt = finf[:, 1024:2048]
            w = [v_wsb] if jt == 0 else []
            em("vector", "tensor_tensor", g2, s_xn[:, jt * 1024:(jt + 1) * 1024],
               s_wsb[:, :], op=ALU.mult, waits=tuple(w))
            em("vector", "tensor_tensor", g2[:, 0:512], g2[:, 0:512],
               ps_a[:, :], op=ALU.add, waits=(t_bm,))
            v_g2[jt] = em("vector", "tensor_tensor", g2[:, 512:1024],
                          g2[:, 512:1024], ps_b[:, :], op=ALU.add,
                          inc=(VE, 1))
            psfree["a"] = v_g2[jt]
            s_si = em("scalar", "activation", gt, g2, AF.Silu,
                      waits=(v_g2[jt],), inc=(SM, 1))
            em("vector", "tensor_tensor", g2, psT[:, :], gt, op=ALU.mult,
               waits=(s_si, t_sT[jt]))
            v_fin[jt] = em("vector", "tensor_tensor",
                           kvf[:, 2048 + half * 1024:2048 + (half + 1) * 1024],
                           g2, kvf[:, half * 1024:(half + 1) * 1024],
                           op=ALU.add, waits=(d_xo,), inc=(VE, 1))
            psfree[key] = v_fin[jt]
            em("sync", "dma_start",
               out=outT[jt * 128:(jt + 1) * 128, :],
               in_=kvf[:, 2048 + half * 1024:2048 + (half + 1) * 1024],
               waits=(v_fin[jt],), inc=(ST, 16))
        em("sync", None, waits=(ST.at(),))

        block = en(nc.Block())

        def replay(engine, key):
            for fn, args, kw, waits, inc in PROG[key]:
                for sw in waits:
                    engine.wait_ge(sw[0], sw[1])
                if fn is None:
                    continue
                ins = getattr(engine, fn)(*args, **kw)
                if inc is not None:
                    ins.then_inc(inc[0], inc[1])

        @block.sync
        def _(engx):
            replay(engx, "sync")

        @block.gpsimd
        def _(engx):
            replay(engx, "gpsimd")

        @block.tensor
        def _(engx):
            replay(engx, "tensor")

        @block.scalar
        def _(engx):
            replay(engx, "scalar")

        @block.vector
        def _(engx):
            replay(engx, "vector")

    return nc


_NC_CACHE = None


def kernel(L0_attn_hidden_states, L0_attn_Wq, L0_attn_Wkv, L0_attn_Wo,
           L0_ffn_router_weight, L0_ffn_router_bias, L0_ffn_expert_bias):
    global _NC_CACHE
    x = np.ascontiguousarray(np.asarray(L0_attn_hidden_states, np.float32)
                             .reshape(S, H))
    Wq = np.asarray(L0_attn_Wq, np.float32)
    Wkv = np.asarray(L0_attn_Wkv, np.float32)
    Wo = np.asarray(L0_attn_Wo, np.float32)
    Rw = np.asarray(L0_ffn_router_weight, np.float32)
    Rb = np.asarray(L0_ffn_router_bias, np.float32)
    Eb = np.asarray(L0_ffn_expert_bias, np.float32)

    wq_t = np.ascontiguousarray(
        Wq.reshape(NHT, 128, NH, 128).transpose(2, 0, 1, 3))
    wo_t = np.ascontiguousarray(
        Wo.reshape(NHT, 128, NHT, 128).transpose(2, 0, 1, 3))
    wkv_t = np.ascontiguousarray(
        Wkv.reshape(NHT, 128, 2, 128).transpose(0, 2, 1, 3))
    rwt_t = np.ascontiguousarray(Rw.T.reshape(NHT, 128, E))
    eb_t = np.ascontiguousarray(Eb.reshape(E, NHT, 128).transpose(1, 0, 2))
    rb_t = np.ascontiguousarray(Rb.reshape(E, 1))
    tri = np.ascontiguousarray(
        np.tril(np.ones((128, 128), np.float32)).T)
    eye = np.eye(128, dtype=np.float32)
    onescol = np.ones((128, 128), np.float32)
    onesrow = np.ones((1, 128), np.float32)

    xT = x.T
    in_maps = []
    own_rows = []
    for c in range(NCORE):
        idx = ((8 * np.arange(8)[:, None] + c) * 128
               + np.arange(128)[None, :]).reshape(-1)
        own_rows.append(idx)
        shift = (7 - c) * 128
        xt_c = np.zeros((H, S), np.float32)
        xt_c[:, shift:] = xT[:, :S - shift]
        xo_c = np.ascontiguousarray(xT[:, idx])
        in_maps.append(dict(
            xt=xt_c, xo=xo_c, wq=wq_t, wo=wo_t, wkv=wkv_t, rwt=rwt_t,
            rbias=rb_t, eb=eb_t, btab=build_btab(c), tri=tri, eye=eye,
            onescol=onescol, onesrow=onesrow,
            epsc=np.full((1, 1), EPS, np.float32)))

    if _NC_CACHE is None:
        _NC_CACHE = build_program()
    nc = _NC_CACHE
    res = run_bass_kernel_spmd(nc, in_maps, core_ids=list(range(NCORE)))

    out = np.empty((S, H), np.float32)
    for c in range(NCORE):
        out[own_rows[c]] = res.results[c]["outT"].T
    return out.reshape(1, S, H)


# version 27
# speedup vs baseline: 1.0219x; 1.0090x over previous
import numpy as np
import ml_dtypes
import concourse.bass as bass
import concourse.mybir as mybir
from concourse.bass_utils import run_bass_kernel_spmd

F32 = mybir.dt.float32
BF16 = mybir.dt.bfloat16
AF = mybir.ActivationFunctionType
ALU = mybir.AluOpType
AX = mybir.AxisListType

S = 8192
H = 4096
NH = 32
HD = 128
E = 64
NCORE = 8
T = 1024
NHT = H // 128
SCALE = HD ** -0.5
CSAFE = 4.0
CUT = 110.0
SPAN = 56.0
EPS = 1e-5
GARBAGE_BIAS = -30000.0

SLOPES = [2.0 ** (-0.25 * (h + 1)) for h in range(NH)]


def _head_mode(h):
    s = SLOPES[h]
    if s * 128.0 > SPAN:
        return 'W64'
    for b in (8, 4, 2):
        if s * ((b - 1) * 1024 + 128) <= SPAN:
            return b
    return 1


def make_schedule():
    heads = []
    for h in range(NH):
        s = SLOPES[h]
        dmax = int((CUT / s + 127) // 128)
        mode = _head_mode(h)
        kts = []
        for kt in range(63, -1, -1):
            ja = max(0, -(-(kt - 7) // 8))
            jb = min(7, (kt + dmax - 7) // 8)
            if ja > jb:
                continue
            batches = []
            if mode == 'W64':
                for j in range(jb, ja - 1, -1):
                    for half in (0, 1):
                        c0 = j * 128 + half * 64
                        ref = (8 * j + 7) * 128 + (half + 1) * 64
                        batches.append((c0, c0 + 64, ref))
            else:
                B = mode
                for gg in reversed(range(8 // B)):
                    g0, g1 = gg * B, gg * B + B - 1
                    lo, hi = max(ja, g0), min(jb, g1)
                    if lo > hi:
                        continue
                    batches.append((lo * 128, (hi + 1) * 128,
                                    (8 * g1 + 7) * 128 + 128))
            diag = (kt % 8 == 7) and (ja <= (kt - 7) // 8 <= jb)
            kts.append((kt, ja, jb, tuple(batches), diag))
        heads.append((h, s, kts))
    return heads


SCHED = make_schedule()
HEAD_NV = [sum(len(b) for (_, _, _, b, _) in kts) for (_, _, kts) in SCHED]
HEAD_OFF = [0]
for nv in HEAD_NV:
    HEAD_OFF.append(HEAD_OFF[-1] + nv)
NVEC = HEAD_OFF[-1]
assert max(HEAD_NV) <= 160


def build_btab(core):
    tab = np.empty((128, NVEC), np.float32)
    p = np.arange(128, dtype=np.float64)
    vi = 0
    for (h, slope, kts) in SCHED:
        for (kt, ja, jb, batches, diag) in kts:
            for (c0, c1, ref) in batches:
                if kt < 7 - core:
                    tab[:, vi] = GARBAGE_BIAS
                else:
                    tab[:, vi] = (slope * (kt * 128 + p - ref) - CSAFE
                                  ).astype(np.float32)
                vi += 1
    assert vi == NVEC
    return tab


class Ctr:
    __slots__ = ("sem", "n")

    def __init__(self, sem):
        self.sem = sem
        self.n = 0

    def at(self):
        return (self.sem, self.n)


def _splits(a, b):
    out = []
    if b > 512:
        out.append((max(a, 512), b))
    if a < 512:
        out.append((a, min(b, 512)))
    return out


def build_program():
    nc = bass.Bass("TRN2", target_bir_lowering=False)

    def din(name, shape):
        return nc.dram_tensor(name, shape, F32, kind="ExternalInput").ap()

    xt = din("xt", [H, S])
    xo = din("xo", [H, T])
    wq = din("wq", [NH, NHT, 128, 128])
    wo = din("wo", [NHT, NHT, 128, 128])
    wkv = din("wkv", [NHT, 2, 128, 128])
    rwt = din("rwt", [NHT, 128, E])
    rbias = din("rbias", [E, 1])
    eb = din("eb", [NHT, E, 128])
    btab = din("btab", [128, NVEC])
    tri = din("tri", [128, 128])
    eye = din("eye", [128, 128])
    onescol = din("onescol", [128, 128])
    onesrow = din("onesrow", [1, 128])
    epsc = din("epsc", [1, 1])
    outT = nc.dram_tensor("outT", [H, T], F32, kind="ExternalOutput").ap()

    from contextlib import ExitStack
    with ExitStack() as ctx:
        en = ctx.enter_context
        s_xn = en(nc.sbuf_tensor("s_xn", [128, NHT * 1024], BF16))
        s_arena = en(nc.sbuf_tensor("s_arena", [128, NHT * 1024], BF16))
        s_kv = en(nc.sbuf_tensor("s_kv", [128, 16384], BF16))
        s_wbuf = en(nc.sbuf_tensor("s_wbuf", [128, 8192], BF16))
        s_pq = en(nc.sbuf_tensor("s_pq", [128, 3072], BF16))
        s_fin = en(nc.sbuf_tensor("s_fin", [128, 4096], BF16))
        s_bias = en(nc.sbuf_tensor("s_bias", [128, 320], F32))
        s_msc = en(nc.sbuf_tensor("s_msc", [64, 1024], BF16))
        s_wsb = en(nc.sbuf_tensor("s_wsb", [128, 1024], F32))
        s_rows = en(nc.sbuf_tensor("s_rows", [1, 1536], F32))
        s_rows_bf = en(nc.sbuf_tensor("s_rows_bf", [1, 1024], BF16))
        s_negS = en(nc.sbuf_tensor("s_negS", [1, 256], BF16))
        s_tri = en(nc.sbuf_tensor("s_tri", [128, 128], BF16))
        s_eyeb = en(nc.sbuf_tensor("s_eyeb", [128, 128], BF16))
        s_eye32 = en(nc.sbuf_tensor("s_eye32", [128, 128], F32))
        s_ones128 = en(nc.sbuf_tensor("s_ones128", [128, 128], BF16))
        s_onesrow = en(nc.sbuf_tensor("s_onesrow", [1, 128], BF16))
        s_onesrow32 = en(nc.sbuf_tensor("s_onesrow32", [1, 128], F32))
        s_rbias = en(nc.sbuf_tensor("s_rbias", [E, 1], F32))
        s_sm = en(nc.sbuf_tensor("s_sm", [128, 14], F32))

        ps_s0 = en(nc.psum_tensor("ps_s0", [128, 1024], F32))
        ps_s1 = en(nc.psum_tensor("ps_s1", [128, 1024], F32))
        ps_a = en(nc.psum_tensor("ps_a", [128, 512], F32))
        ps_b = en(nc.psum_tensor("ps_b", [128, 512], F32))
        ps_m0 = en(nc.psum_tensor("ps_m0", [128, 512], F32))
        ps_m1 = en(nc.psum_tensor("ps_m1", [128, 512], F32))

        sems = {}
        for nm in ("g", "d", "st", "tS", "tA", "tQ", "sE", "sM", "vE"):
            sems[nm] = en(nc.semaphore("sem_" + nm))
        G = Ctr(sems["g"]); D = Ctr(sems["d"]); ST = Ctr(sems["st"])
        TS = Ctr(sems["tS"]); TA = Ctr(sems["tA"]); TQ = Ctr(sems["tQ"])
        SE = Ctr(sems["sE"]); SM = Ctr(sems["sM"]); VE = Ctr(sems["vE"])

        PROG = {"sync": [], "gpsimd": [], "tensor": [], "scalar": [], "vector": []}

        def em(eng, fn, *args, waits=(), inc=None, **kw):
            rec_inc = None
            ret = None
            if inc is not None:
                ctr, amt = inc
                ctr.n += amt
                rec_inc = (ctr.sem, amt)
                ret = (ctr.sem, ctr.n)
            PROG[eng].append((fn, args, kw, tuple(w for w in waits if w), rec_inc))
            return ret

        pqf = s_pq[:, :].bitcast(F32)
        finf = s_fin[:, :].bitcast(F32)
        kvf = s_kv[:, :].bitcast(F32)
        sbiasb = s_bias[:, :].bitcast(BF16)
        wsbb = s_wsb[:, :].bitcast(BF16)
        ps_s1b = ps_s1[:, :].bitcast(BF16)
        ps_tpb = ps_s0[:, 640:896].bitcast(BF16)
        ps_bb = ps_b[:, :].bitcast(BF16)

        em("gpsimd", "dma_start", out=s_eyeb[:, :], in_=eye[:, :], inc=(G, 16))
        em("gpsimd", "dma_start", out=s_tri[:, :], in_=tri[:, :], inc=(G, 16))
        em("gpsimd", "dma_start", out=s_ones128[:, :], in_=onescol[:, :], inc=(G, 16))
        em("gpsimd", "dma_start", out=s_onesrow[:, :], in_=onesrow[:, :], inc=(G, 16))
        g_wkv = em("gpsimd", "dma_start",
                   out=s_wbuf.rearrange("p (a b j) -> p a b j", b=2, j=128),
                   in_=wkv.rearrange("a b p j -> p a b j"), inc=(G, 16))
        em("sync", "dma_start", out=s_eye32[:, :], in_=eye[:, :], inc=(D, 16))
        em("sync", "dma_start", out=s_onesrow32[:, :], in_=onesrow[:, :], inc=(D, 16))
        em("sync", "dma_start", out=s_rbias[:, :], in_=rbias[:, :], inc=(D, 16))
        d_consts = em("sync", "dma_start", out=s_sm[0:1, 3:4], in_=epsc[:, :],
                      inc=(D, 16))

        for jj in range(2):
            for ht in range(NHT):
                w = (g_wkv,) if (jj == 0 and ht == 0) else ()
                last = (jj == 1 and ht == NHT - 1)
                em("tensor", "matmul", ps_a[:, jj:jj + 1],
                   s_wbuf[:, ht * 256 + jj * 128:ht * 256 + jj * 128 + 128],
                   s_ones128[:, 0:1], start=(ht == 0), stop=(ht == NHT - 1),
                   waits=w, inc=(TQ, 1) if last else None)
        t_cs = TQ.at()
        v_cs = em("vector", "tensor_copy", sbiasb[:, 0:2], ps_a[:, 0:2],
                  waits=(t_cs,), inc=(VE, 1))
        for jj in range(2):
            em("tensor", "transpose", ps_bb[0:1, jj * 128:(jj + 1) * 128],
               sbiasb[:, jj:jj + 1], s_eyeb[:, :],
               waits=(v_cs,) if jj == 0 else (),
               inc=(TQ, 1) if jj == 1 else None)
        t_tp = TQ.at()
        for jj in range(2):
            em("scalar", "activation", s_negS[0:1, jj * 128:(jj + 1) * 128],
               ps_bb[0:1, jj * 128:(jj + 1) * 128], AF.Copy, scale=-1.0,
               waits=(t_tp,) if jj == 0 else (),
               inc=(SM, 1) if jj == 1 else None)
        s_negS_done = SM.at()

        NCH = 16
        c_mean = s_rows[0:1, 0:512]
        c_var = s_rows[0:1, 512:1024]
        c_rstd = s_rows[0:1, 1024:1536]
        r_meanb = s_rows_bf[0:1, 0:512]
        r_rstdb = s_rows_bf[0:1, 512:1024]
        chunk_mm_done = [None] * NCH
        chunk_sq_done = [None] * NCH
        chunk_ev_done = [None] * NCH

        def xt_tile(ci, ht):
            base = (ci % 2) * 16384 + (ht // 8) * 4096 + (ht % 8) * 512
            return s_arena[:, base:base + 512]

        xn_slot_done = [None] * 8
        pending_norm = []
        d_xo = None
        last_grp_t = None
        sq_grp_t = [None, None]
        stats_free = [None, None]
        kv_free = None
        pending_vt = []
        row_free = None

        def emit_norm(j):
            for ht in range(NHT):
                xns = s_xn[:, ht * 1024 + j * 128:ht * 1024 + (j + 1) * 128]
                w = (d_xo,) if ht == 0 else ()
                em("vector", "tensor_tensor", xns, xns,
                   pqf[:, j * 128:(j + 1) * 128], op=ALU.mult, waits=w)
                em("vector", "tensor_tensor", xns, xns,
                   finf[:, 1024 + j * 128:1024 + (j + 1) * 128],
                   op=ALU.subtract)
            em("vector", "tensor_copy", s_sm[:, 5:6], s_sm[:, 5:6],
               inc=(VE, 1))
            xn_slot_done[j] = VE.at()

        for ci in range(NCH):
            w0 = []
            if ci >= 2:
                w0 = [chunk_mm_done[ci - 2], chunk_sq_done[ci - 2]]
            for g4 in range(4):
                src = xt[g4 * 1024:(g4 + 1) * 1024, ci * 512:(ci + 1) * 512]
                src = src.rearrange("(ht p) t -> p ht t", p=128)
                dst = s_arena[:, (ci % 2) * 16384 + g4 * 4096:
                              (ci % 2) * 16384 + (g4 + 1) * 4096]
                dst = dst.rearrange("p (ht t) -> p ht t", t=512)
                em("gpsimd", "dma_start", out=dst, in_=src,
                   waits=tuple(w0) if g4 == 0 else (), inc=(G, 16))
            g_x = G.at()
            if ci == 1:
                for gg in range(2):
                    src = xo[gg * 2048:(gg + 1) * 2048, :].rearrange(
                        "(ht p) t -> p ht t", p=128)
                    dst = s_xn[:, gg * 16384:(gg + 1) * 16384].rearrange(
                        "p (ht t) -> p ht t", t=1024)
                    em("gpsimd", "dma_start", out=dst, in_=src, inc=(G, 16))
                d_xo = G.at()

            pmean = ps_m0 if ci % 2 == 0 else ps_s1[:, 0:512]
            psum2 = ps_m1 if ci % 2 == 0 else ps_s1[:, 512:1024]
            tw = [g_x]
            if stats_free[ci % 2] is not None:
                tw.append(stats_free[ci % 2])
            for ht in range(NHT):
                em("tensor", "matmul", pmean[:, :], s_ones128[:, :],
                   xt_tile(ci, ht), start=(ht == 0), stop=(ht == NHT - 1),
                   waits=tuple(tw) if ht == 0 else (),
                   inc=(TQ, 1) if ht == NHT - 1 else None)
            t_mean = TQ.at()

            mw = [t_mean]
            if row_free is not None:
                mw.append(row_free)
            s_mean = em("scalar", "activation", c_mean, pmean[0:1, :], AF.Copy,
                        scale=1.0 / H, waits=tuple(mw), inc=(SM, 1))
            v_meanb = em("vector", "tensor_copy", r_meanb, c_mean,
                         waits=(s_mean,), inc=(VE, 1))

            for grp in range(16):
                pair = grp % 2
                for k2 in range(2):
                    ht = grp * 2 + k2
                    w = []
                    if k2 == 0:
                        w = [g_x] if grp == 0 else []
                        if sq_grp_t[pair] is not None:
                            w.append(sq_grp_t[pair])
                    em("vector", "tensor_mul",
                       wsbb[:, (pair * 2 + k2) * 512:(pair * 2 + k2 + 1) * 512],
                       xt_tile(ci, ht), xt_tile(ci, ht),
                       waits=tuple(w), inc=(VE, 1) if k2 == 1 else None)
                v_sq = VE.at()
                for k2 in range(2):
                    ht = grp * 2 + k2
                    em("tensor", "matmul", psum2[:, :], s_ones128[:, :],
                       wsbb[:, (pair * 2 + k2) * 512:(pair * 2 + k2 + 1) * 512],
                       start=(ht == 0), stop=(ht == NHT - 1),
                       waits=(v_sq,) if k2 == 0 else (),
                       inc=(TQ, 1) if k2 == 1 else None)
                sq_grp_t[pair] = TQ.at()
            chunk_sq_done[ci] = VE.at()
            t_sumsq = sq_grp_t[15 % 2]

            for (cj, vkv_j, evd_j) in pending_vt:
                for kk in range(4):
                    em("tensor", "transpose", ps_tpb[:, kk * 128:(kk + 1) * 128],
                       sbiasb[:, kk * 128:(kk + 1) * 128], s_eyeb[:, :],
                       waits=(vkv_j, evd_j) if kk == 0 else (),
                       inc=(TQ, 1) if kk == 3 else None)
                t_vt = TQ.at()
                for kk in range(4):
                    em("vector", "tensor_copy",
                       s_kv[:, 8192 + (4 * cj + kk) * 128:
                            8192 + (4 * cj + kk + 1) * 128],
                       ps_tpb[:, kk * 128:(kk + 1) * 128],
                       waits=(t_vt,) if kk == 0 else ())
            pending_vt.clear()

            for jj, pdst in ((0, ps_a), (1, ps_b)):
                for ht in range(NHT):
                    w = []
                    if jj == 0 and ht == 0:
                        if kv_free is not None:
                            w.append(kv_free)
                        if ci == 0:
                            w.append(g_wkv)
                    em("tensor", "matmul", pdst[:, :],
                       s_wbuf[:, ht * 256 + jj * 128:ht * 256 + jj * 128 + 128],
                       xt_tile(ci, ht), start=(ht == 0), stop=False,
                       waits=tuple(w))
                em("tensor", "matmul", pdst[:, :],
                   s_negS[0:1, jj * 128:(jj + 1) * 128], r_meanb,
                   start=False, stop=True,
                   waits=(v_meanb, s_negS_done) if jj == 0 else (),
                   inc=(TQ, 1) if jj == 1 else None)
            t_kv = TQ.at()
            chunk_mm_done[ci] = t_kv

            em("scalar", "activation", c_var, psum2[0:1, :], AF.Copy,
               scale=1.0 / H, waits=(t_sumsq,))
            s_v1 = em("scalar", "activation", s_rows[0:1, 0:1],
                      s_rows[0:1, 0:1], AF.Copy, inc=(SM, 1))
            stats_free[ci % 2] = s_v1
            em("vector", "tensor_mul", c_rstd, c_mean, c_mean, waits=(s_mean,))
            v_v1 = em("vector", "tensor_sub", c_var, c_var, c_rstd,
                      waits=(s_v1,), inc=(VE, 1))
            em("scalar", "activation", c_rstd, c_var, AF.Ln,
               bias=s_sm[0:1, 3:4], waits=(v_v1, d_consts))
            s_rstd = em("scalar", "activation", c_rstd, c_rstd, AF.Exp,
                        scale=-0.5, inc=(SM, 1))
            v_rstdb = em("vector", "tensor_copy", r_rstdb, c_rstd,
                         waits=(s_rstd,), inc=(VE, 1))

            t_rb = em("tensor", "matmul", ps_s0[:, 0:512], s_onesrow[:, :],
                      r_rstdb, start=True, stop=True, waits=(v_rstdb,),
                      inc=(TQ, 1))
            v_rb = em("vector", "tensor_copy", finf[:, 0:512], ps_s0[:, 0:512],
                      waits=(t_rb,), inc=(VE, 1))

            em("vector", "tensor_tensor", s_kv[:, ci * 512:(ci + 1) * 512],
               ps_a[:, :], finf[:, 0:512], op=ALU.mult, waits=(t_kv,))
            v_kv = em("vector", "tensor_tensor", sbiasb[:, 0:512],
                      ps_b[:, :], finf[:, 0:512], op=ALU.mult, inc=(VE, 1))
            kv_free = v_kv

            pending_vt.append((ci, v_kv, None))

            for r in range(4):
                g4i = 4 * ci + r
                if g4i % 8 != 7:
                    continue
                j = g4i // 8
                v_ms = em("vector", "tensor_mul", c_var, c_mean, c_rstd,
                          inc=(VE, 1))
                t_ms = em("tensor", "matmul", ps_s0[:, 512:640],
                          s_onesrow32[:, :], c_var[0:1, r * 128:(r + 1) * 128],
                          start=True, stop=True, waits=(v_ms, d_consts),
                          inc=(TQ, 1))
                if ci == NCH - 1:
                    for ht in range(NHT):
                        xns = s_xn[:, ht * 1024 + j * 128:
                                   ht * 1024 + (j + 1) * 128]
                        w = (t_ms, d_xo) if ht == 0 else ()
                        em("vector", "tensor_tensor", xns, xns,
                           ps_s0[:, r * 128:(r + 1) * 128], op=ALU.mult,
                           waits=w)
                        em("vector", "tensor_tensor", xns, xns,
                           ps_s0[:, 512:640], op=ALU.subtract)
                    em("vector", "tensor_copy", s_sm[:, 5:6], s_sm[:, 5:6],
                       inc=(VE, 1))
                    xn_slot_done[j] = VE.at()
                else:
                    em("vector", "tensor_copy", pqf[:, j * 128:(j + 1) * 128],
                       finf[:, r * 128:(r + 1) * 128])
                    em("vector", "tensor_copy",
                       finf[:, 1024 + j * 128:1024 + (j + 1) * 128],
                       ps_s0[:, 512:640], waits=(t_ms,))
                    pending_norm.append((j, ci))
            chunk_ev_done[ci] = em("vector", "tensor_copy", s_sm[:, 5:6],
                                   s_sm[:, 5:6], inc=(VE, 1))
            row_free = chunk_ev_done[ci]
            pending_vt[-1] = (pending_vt[-1][0], pending_vt[-1][1],
                              chunk_ev_done[ci])
            keep = [p for p in pending_norm if p[1] > ci - 2]
            for j, _cc in pending_norm:
                if _cc <= ci - 2:
                    emit_norm(j)
            pending_norm[:] = keep

        for (cj, vkv_j, evd_j) in pending_vt:
            for kk in range(4):
                em("tensor", "transpose", ps_tpb[:, kk * 128:(kk + 1) * 128],
                   sbiasb[:, kk * 128:(kk + 1) * 128], s_eyeb[:, :],
                   waits=(vkv_j, evd_j, xn_slot_done[7]) if kk == 0 else (),
                   inc=(TQ, 1) if kk == 3 else None)
            t_vt = TQ.at()
            for kk in range(4):
                em("vector", "tensor_copy",
                   s_kv[:, 8192 + (4 * cj + kk) * 128:
                        8192 + (4 * cj + kk + 1) * 128],
                   ps_tpb[:, kk * 128:(kk + 1) * 128],
                   waits=(t_vt,) if kk == 0 else ())
        pending_vt.clear()
        for j, _cc in pending_norm:
            emit_norm(j)
        pending_norm.clear()
        p1_t_done = TQ.at()
        p1_v_done = VE.at()
        p1_s_done = SM.at()
        v_xn_all = p1_v_done

        v_xn = p1_v_done

        SU = [ps_s0[:, 0:512], ps_s0[:, 512:1024], ps_s1[:, 0:512]]
        ps_q = ps_s1[:, 512:1024]
        PU = [s_pq[:, u * 512:(u + 1) * 512] for u in range(3)]
        qtbuf = [s_pq[:, 1536:2560], s_fin[:, 2048:3072],
                 s_fin[:, 3072:4096]]
        psq_free = [p1_s_done]
        qp_mm_done = [None] * NH
        qp_evac_done = [None] * NH

        class QF:

            def __init__(self, hq, g_wq):
                self.hq = hq
                self.g_wq = g_wq
                self.idx = 0
                self.ev0 = None

            def pop(self):
                if self.idx >= 64:
                    return False
                chunk, ht = self.idx // 32, self.idx % 32
                half = self.hq % 2
                w = []
                if self.idx == 0:
                    w = [self.g_wq, psq_free[0], v_xn]
                elif self.idx == 32:
                    w = [self.ev0]
                em("tensor", "matmul", ps_q,
                   s_wbuf[:, half * 4096 + ht * 128:half * 4096 + (ht + 1) * 128],
                   s_xn[:, ht * 1024 + chunk * 512:ht * 1024 + chunk * 512 + 512],
                   start=(ht == 0), stop=(ht == NHT - 1), waits=tuple(w),
                   inc=(TQ, 1) if ht == NHT - 1 else None)
                self.idx += 1
                if self.idx in (32, 64):
                    tq = TQ.at()
                    ve = em("vector", "tensor_copy",
                            qtbuf[self.hq % 3][:, chunk * 512:(chunk + 1) * 512],
                            ps_q, waits=(tq,), inc=(VE, 1))
                    if self.idx == 32:
                        self.ev0 = ve
                    else:
                        qp_mm_done[self.hq] = tq
                        qp_evac_done[self.hq] = ve
                        psq_free[0] = ve
                return True

        def mk_wq_dma(hq):
            half = hq % 2
            gw = [p1_t_done] if hq < 2 else [qp_mm_done[hq - 2]]
            return em("gpsimd", "dma_start",
                      out=s_wbuf[:, half * 4096:(half + 1) * 4096].rearrange(
                          "p (a d) -> p a d", d=128),
                      in_=wq[hq].rearrange("a p d -> p a d"),
                      waits=tuple(gw), inc=(G, 16))

        g_rwt = em("gpsimd", "dma_start",
                   out=s_pq[:, 0:2048].rearrange("p (a e) -> p a e", e=E),
                   in_=rwt.rearrange("a p e -> p a e"),
                   waits=(xn_slot_done[6],), inc=(G, 16))
        for chunk, pdst in ((0, ps_m0), (1, ps_m1)):
            for ht in range(NHT):
                w = (g_rwt, v_xn, p1_t_done, p1_s_done) if (chunk == 0 and ht == 0) else ()
                em("tensor", "matmul", pdst[0:64, :],
                   s_pq[:, ht * 64:(ht + 1) * 64],
                   s_xn[:, ht * 1024 + chunk * 512:ht * 1024 + chunk * 512 + 512],
                   start=(ht == 0), stop=(ht == NHT - 1), waits=w,
                   inc=(TQ, 1) if (chunk == 1 and ht == NHT - 1) else None)
        t_lg = TQ.at()
        em("vector", "tensor_scalar", finf[0:64, 0:512], ps_m0[0:64, :],
           s_rbias[:, :], None, ALU.add, waits=(t_lg, d_consts))
        s_lg = em("vector", "tensor_scalar", finf[0:64, 512:1024], ps_m1[0:64, :],
                  s_rbias[:, :], None, ALU.add, inc=(VE, 1))

        lgN = s_bias[:, 0:64]
        prb = s_bias[:, 64:128]
        mkb = s_bias[:, 128:192]
        mscN = sbiasb[:, 512:576]
        negmax = s_sm[:, 0:1]
        Zsum = s_sm[:, 4:5]
        Zr = s_sm[:, 1:2]
        wsumc = s_sm[:, 2:3]
        top8 = s_sm[:, 6:14]
        prev_v4 = None
        for ti in range(8):
            w = (s_lg,) if ti == 0 else (prev_v4,)
            t_1 = em("tensor", "transpose", ps_a[:, 0:64],
                     finf[0:64, ti * 128:(ti + 1) * 128], s_eye32[0:64, 0:64],
                     waits=w, inc=(TQ, 1))
            em("vector", "tensor_copy", lgN, ps_a[:, 0:64], waits=(t_1,))
            v_1 = em("vector", "reduce_max", negmax, lgN, axis=AX.X,
                     negate=True, inc=(VE, 1))
            s_1 = em("scalar", "activation", prb, lgN, AF.Exp, bias=negmax,
                     accum_out=Zsum, waits=(v_1,), inc=(SM, 1))
            em("vector", "max", out=top8, in_=prb, waits=(s_1,))
            em("vector", "reciprocal", Zr, Zsum)
            em("vector", "tensor_scalar", mkb, prb, top8[:, 7:8], None,
               ALU.is_ge)
            em("vector", "tensor_mul", mkb, mkb, prb)
            v_3 = em("vector", "tensor_scalar", mscN, mkb, Zr, 0.0, ALU.mult,
                     ALU.add, accum_out=wsumc, inc=(VE, 1))
            t_2a = em("tensor", "transpose", ps_bb[0:64, 0:128], mscN,
                      s_eyeb[:, :], waits=(v_3,), inc=(TQ, 1))
            t_2 = em("tensor", "transpose", ps_a[0:1, 64:192], wsumc,
                     s_eye32[:, :], inc=(TQ, 1))
            em("vector", "tensor_copy", s_msc[0:64, ti * 128:(ti + 1) * 128],
               ps_bb[0:64, 0:128], waits=(t_2,))
            prev_v4 = em("vector", "tensor_copy",
                         s_rows[0:1, ti * 128:(ti + 1) * 128],
                         ps_a[0:1, 64:192], inc=(VE, 1))
        em("tensor", "matmul", ps_s0[:, 0:512], s_onesrow32[:, :],
           s_rows[0:1, 0:512], start=True, stop=True, waits=(prev_v4,))
        t_wsb = em("tensor", "matmul", ps_s0[:, 512:1024], s_onesrow32[:, :],
                   s_rows[0:1, 512:1024], start=True, stop=True, inc=(TQ, 1))
        v_wsb = em("vector", "tensor_copy", s_wsb[:, :], ps_s0[:, :],
                   waits=(t_wsb,), inc=(VE, 1))

        unit_free = [v_wsb, v_wsb, p1_s_done]
        probs_free = [None, None, None]
        mm_free = [s_lg, s_lg]
        ab_free = [v_wsb, v_wsb]
        head_exp_done = [None] * NH
        ta_head_done = [None] * NH
        v_ev = [None] * NH
        tail_ev = [[None, None] for _ in range(NH)]

        qf = QF(0, mk_wq_dma(0))
        while qf.pop():
            pass
        next_qf = QF(1, mk_wq_dma(1))

        all_tasks = []
        meta = {}
        for (hh, slope, kts) in SCHED:
            tasks = []
            vi_base = 0
            for (kt, ja, jb, batches, diag) in kts:
                jd = (kt - 7) // 8 if diag else None
                for (a, b) in _splits(ja * 128, (jb + 1) * 128):
                    tb = []
                    for bi, (c0, c1, ref) in enumerate(batches):
                        lo_, hi_ = max(c0, a), min(c1, b)
                        if lo_ < hi_:
                            tb.append((lo_, hi_, vi_base + bi))
                    hd = jd if (jd is not None and a <= jd * 128 < b) else None
                    tasks.append((kt, a, b, tuple(tb), hd))
                vi_base += len(batches)
            assert vi_base == HEAD_NV[hh]
            lo_idx = [i for i, t in enumerate(tasks) if t[1] < 512]
            hi_idx = [i for i, t in enumerate(tasks) if t[1] >= 512]
            meta[hh] = dict(
                first_lo=min(lo_idx), last_lo=max(lo_idx),
                first_hi=min(hi_idx) if hi_idx else None,
                last_hi=max(hi_idx) if hi_idx else None,
                n=len(tasks))
            for i, t in enumerate(tasks):
                all_tasks.append((hh,) + t + (i,))

        NG = len(all_tasks)
        ts_g = [None] * NG
        se_g = [None] * NG
        vmk_g = [None] * NG
        d_bias_h = [None] * NH

        def ensure_bias(hh):
            if d_bias_h[hh] is None:
                half = hh % 2
                nv = HEAD_NV[hh]
                dw = [v_wsb] if hh < 2 else [head_exp_done[hh - 2]]
                d_bias_h[hh] = em("sync", "dma_start",
                                  out=s_bias[:, half * 160:half * 160 + nv],
                                  in_=btab[:, HEAD_OFF[hh]:HEAD_OFF[hh] + nv],
                                  waits=tuple(dw), inc=(D, 16))

        def emit_scores_g(g):
            hh, kt, a, b, tb, hd, loc = all_tasks[g]
            ensure_bias(hh)
            u = g % 3
            w = [unit_free[u]]
            if loc < 3:
                w.append(qp_evac_done[hh])
            ts_g[g] = em("tensor", "matmul", SU[u][:, 0:b - a],
                         s_kv[:, kt * 128:(kt + 1) * 128],
                         qtbuf[hh % 3][:, a:b],
                         start=True, stop=True, waits=tuple(w), inc=(TS, 1))

        def emit_exp_g(g):
            hh, kt, a, b, tb, hd, loc = all_tasks[g]
            u = g % 3
            half = hh % 2
            w = [ts_g[g], d_bias_h[hh]]
            if probs_free[u] is not None:
                w.append(probs_free[u])
            for bi, (c0, c1, vi) in enumerate(tb):
                em("scalar", "activation", PU[u][:, c0 - a:c1 - a],
                   SU[u][:, c0 - a:c1 - a], AF.Exp,
                   bias=s_bias[:, half * 160 + vi:half * 160 + vi + 1],
                   scale=SCALE,
                   waits=tuple(w) if bi == 0 else (),
                   inc=(SE, 1) if bi == len(tb) - 1 else None)
            se_g[g] = SE.at()
            unit_free[u] = se_g[g]

        def emit_mask_g(g):
            hh, kt, a, b, tb, hd, loc = all_tasks[g]
            if hd is None:
                return
            u = g % 3
            c = hd * 128 - a
            vmk_g[g] = em("vector", "tensor_mul", PU[u][:, c:c + 128],
                          PU[u][:, c:c + 128], s_tri[:, :],
                          waits=(se_g[g],), inc=(VE, 1))

        def emit_av_g(g):
            hh, kt, a, b, tb, hd, loc = all_tasks[g]
            u = g % 3
            m = meta[hh]
            lo = a < 512
            w = [se_g[g]]
            if vmk_g[g] is not None:
                w.append(vmk_g[g])
            first = (loc == m["first_lo"]) if lo else (loc == m["first_hi"])
            last = (loc == m["last_lo"]) if lo else (loc == m["last_hi"])
            if first:
                w.append(ab_free[0 if lo else 1])
                mf = mm_free[0 if lo else 1]
                if isinstance(mf, tuple) and isinstance(mf[0], tuple):
                    w.extend(mf)
                else:
                    w.append(mf)
            pd = ps_a if lo else ps_b
            pm = ps_m0 if lo else ps_m1
            em("tensor", "matmul", pd[:, a % 512:a % 512 + (b - a)],
               s_kv[:, 8192 + kt * 128:8192 + (kt + 1) * 128],
               PU[u][:, 0:b - a], start=first, stop=last, waits=tuple(w))
            em("tensor", "matmul", pm[:, a % 512:a % 512 + (b - a)],
               s_ones128[:, :], PU[u][:, 0:b - a],
               start=first, stop=last, inc=(TA, 1))
            probs_free[u] = TA.at()
            if last:
                emit_tail_bank(hh, 0 if lo else 1)

        def emit_tail_bank(hh, bank):
            ta_now = TA.at()
            pm = ps_m0 if bank == 0 else ps_m1
            pd = ps_a if bank == 0 else ps_b
            fr = finf[:, 0:512] if bank == 0 else finf[:, 512:1024]
            em("vector", "reciprocal", fr, pm[:, :], waits=(ta_now,))
            vr = em("vector", "tensor_tensor",
                    s_arena[:, hh * 1024 + bank * 512:
                            hh * 1024 + (bank + 1) * 512],
                    pd[:, :], fr, op=ALU.mult, inc=(VE, 1))
            ab_free[bank] = vr
            mm_free[bank] = vr
            tail_ev[hh][bank] = vr

        def emit_tail(hh):
            head_exp_done[hh] = SE.at()
            ta_head_done[hh] = TA.at()
            assert tail_ev[hh][0] is not None and tail_ev[hh][1] is not None
            v_ev[hh] = max(tail_ev[hh][0], tail_ev[hh][1],
                           key=lambda t: t[1])

        qf = next_qf
        next_qf = QF(2, mk_wq_dma(2))

        def lookahead(g):
            nonlocal qf, next_qf
            hh = all_tasks[g][0]
            cur = all_tasks[g - 3][0] if g >= 3 else all_tasks[0][0]
            if hh != cur and qf is not None and qf.hq == hh:
                while qf.pop():
                    pass
                qf = next_qf if next_qf is not None and next_qf.hq < NH else None
                if qf is not None and hh + 2 < NH and qf.hq == hh + 1:
                    next_qf = QF(hh + 2, mk_wq_dma(hh + 2))
                else:
                    next_qf = None
            emit_scores_g(g)
            emit_exp_g(g)
            emit_mask_g(g)

        for g in range(min(3, NG)):
            emit_scores_g(g)
            emit_exp_g(g)
            emit_mask_g(g)
        head_last = {}
        for gl, tl in enumerate(all_tasks):
            head_last[tl[0]] = gl
        for g in range(NG):
            hh = all_tasks[g][0]
            loc = all_tasks[g][6]
            emit_av_g(g)
            if loc == meta[hh]["n"] - 1:
                emit_tail(hh)
            if g + 3 < NG:
                lookahead(g + 3)
            near_end = g >= head_last[hh] - 11
            if qf is not None and qf.hq == hh + 1:
                if not qf.pop() and next_qf is not None:
                    qf = next_qf
                    if hh + 3 < NH:
                        next_qf = QF(hh + 3, mk_wq_dma(hh + 3))
                    else:
                        next_qf = None
                    if near_end:
                        qf.pop()
            elif qf is not None and qf.hq == hh + 2 and near_end:
                qf.pop()

        attn_done = ta_head_done[NH - 1]
        psfree = {"s0": unit_free[1], "s1": psq_free[0],
                  "a": v_ev[NH - 1], "b": v_ev[NH - 1]}
        p5_s0_extra = unit_free[0]
        p5_s1_extra = unit_free[2]
        qproj_done = qp_mm_done

        t_sT = [None] * NHT
        v_fin = [None] * NHT
        v_g2 = [None] * NHT
        g_wo_pre = [None, None]
        for jt in range(2):
            g_wo_pre[jt] = em("gpsimd", "dma_start",
                              out=s_wbuf[:, jt * 4096:(jt + 1) * 4096].rearrange(
                                  "p (a d) -> p a d", d=128),
                              in_=wo[jt].rearrange("a p d -> p a d"),
                              waits=(qproj_done[NH - 2 + jt],), inc=(G, 16))
        g_eb = em("gpsimd", "dma_start",
                  out=s_kv[0:64, 8192:12288].rearrange("p (a j) -> p a j", j=128),
                  in_=eb.rearrange("a p j -> p a j"),
                  waits=(attn_done,), inc=(G, 16))
        for jt in range(NHT):
            half = jt % 2
            if jt < 2:
                g_wo = g_wo_pre[jt]
            else:
                g_wo = em("gpsimd", "dma_start",
                          out=s_wbuf[:, half * 4096:(half + 1) * 4096].rearrange(
                              "p (a d) -> p a d", d=128),
                          in_=wo[jt].rearrange("a p d -> p a d"),
                          waits=(t_sT[jt - 2],), inc=(G, 16))
            dxw = [attn_done] if jt < 2 else [v_fin[jt - 2]]
            d_xo = em("sync", "dma_start",
                      out=kvf[:, half * 1024:(half + 1) * 1024],
                      in_=xo[jt * 128:(jt + 1) * 128, :],
                      waits=tuple(dxw), inc=(D, 16))
            psT = ps_s0 if half == 0 else ps_s1
            key = "s0" if half == 0 else "s1"
            for chunk in range(2):
                for dtt in range(NHT):
                    w = []
                    if chunk == 0 and dtt == 0:
                        w = [g_wo, psfree[key]]
                        if jt == 0:
                            w.append(v_ev[NH - 1])
                            w.append(p5_s0_extra)
                        if jt == 1:
                            w.append(p5_s1_extra)
                    em("tensor", "matmul",
                       psT[:, chunk * 512:(chunk + 1) * 512],
                       s_wbuf[:, half * 4096 + dtt * 128:half * 4096 + (dtt + 1) * 128],
                       s_arena[:, dtt * 1024 + chunk * 512:dtt * 1024 + chunk * 512 + 512],
                       start=(dtt == 0), stop=(dtt == NHT - 1), waits=tuple(w),
                       inc=(TQ, 1) if (chunk == 1 and dtt == NHT - 1) else None)
            t_sT[jt] = TQ.at()
            bw = [g_eb] if jt == 0 else [v_g2[jt - 1]]
            bw.append(psfree["a"])
            em("tensor", "matmul", ps_a[:, :],
               s_kv[0:64, 8192 + jt * 128:8192 + (jt + 1) * 128],
               s_msc[0:64, 0:512], start=True, stop=True, waits=tuple(bw))
            t_bm = em("tensor", "matmul", ps_b[:, :],
                      s_kv[0:64, 8192 + jt * 128:8192 + (jt + 1) * 128],
                      s_msc[0:64, 512:1024], start=True, stop=True,
                      inc=(TQ, 1))
            psfree["a"] = None

            g2 = finf[:, 0:1024]
            gt = finf[:, 1024:2048]
            w = [v_wsb] if jt == 0 else []
            em("vector", "tensor_tensor", g2, s_xn[:, jt * 1024:(jt + 1) * 1024],
               s_wsb[:, :], op=ALU.mult, waits=tuple(w))
            em("vector", "tensor_tensor", g2[:, 0:512], g2[:, 0:512],
               ps_a[:, :], op=ALU.add, waits=(t_bm,))
            v_g2[jt] = em("vector", "tensor_tensor", g2[:, 512:1024],
                          g2[:, 512:1024], ps_b[:, :], op=ALU.add,
                          inc=(VE, 1))
            psfree["a"] = v_g2[jt]
            s_si = em("scalar", "activation", gt, g2, AF.Silu,
                      waits=(v_g2[jt],), inc=(SM, 1))
            em("vector", "tensor_tensor", g2, psT[:, :], gt, op=ALU.mult,
               waits=(s_si, t_sT[jt]))
            v_fin[jt] = em("vector", "tensor_tensor",
                           kvf[:, 2048 + half * 1024:2048 + (half + 1) * 1024],
                           g2, kvf[:, half * 1024:(half + 1) * 1024],
                           op=ALU.add, waits=(d_xo,), inc=(VE, 1))
            psfree[key] = v_fin[jt]
            em("sync", "dma_start",
               out=outT[jt * 128:(jt + 1) * 128, :],
               in_=kvf[:, 2048 + half * 1024:2048 + (half + 1) * 1024],
               waits=(v_fin[jt],), inc=(ST, 16))
        em("sync", None, waits=(ST.at(),))

        block = en(nc.Block())

        def replay(engine, key):
            for fn, args, kw, waits, inc in PROG[key]:
                for sw in waits:
                    engine.wait_ge(sw[0], sw[1])
                if fn is None:
                    continue
                ins = getattr(engine, fn)(*args, **kw)
                if inc is not None:
                    ins.then_inc(inc[0], inc[1])

        @block.sync
        def _(engx):
            replay(engx, "sync")

        @block.gpsimd
        def _(engx):
            replay(engx, "gpsimd")

        @block.tensor
        def _(engx):
            replay(engx, "tensor")

        @block.scalar
        def _(engx):
            replay(engx, "scalar")

        @block.vector
        def _(engx):
            replay(engx, "vector")

    return nc


_NC_CACHE = None


def kernel(L0_attn_hidden_states, L0_attn_Wq, L0_attn_Wkv, L0_attn_Wo,
           L0_ffn_router_weight, L0_ffn_router_bias, L0_ffn_expert_bias):
    global _NC_CACHE
    x = np.ascontiguousarray(np.asarray(L0_attn_hidden_states, np.float32)
                             .reshape(S, H))
    Wq = np.asarray(L0_attn_Wq, np.float32)
    Wkv = np.asarray(L0_attn_Wkv, np.float32)
    Wo = np.asarray(L0_attn_Wo, np.float32)
    Rw = np.asarray(L0_ffn_router_weight, np.float32)
    Rb = np.asarray(L0_ffn_router_bias, np.float32)
    Eb = np.asarray(L0_ffn_expert_bias, np.float32)

    wq_t = np.ascontiguousarray(
        Wq.reshape(NHT, 128, NH, 128).transpose(2, 0, 1, 3))
    wo_t = np.ascontiguousarray(
        Wo.reshape(NHT, 128, NHT, 128).transpose(2, 0, 1, 3))
    wkv_t = np.ascontiguousarray(
        Wkv.reshape(NHT, 128, 2, 128).transpose(0, 2, 1, 3))
    rwt_t = np.ascontiguousarray(Rw.T.reshape(NHT, 128, E))
    eb_t = np.ascontiguousarray(Eb.reshape(E, NHT, 128).transpose(1, 0, 2))
    rb_t = np.ascontiguousarray(Rb.reshape(E, 1))
    tri = np.ascontiguousarray(
        np.tril(np.ones((128, 128), np.float32)).T)
    eye = np.eye(128, dtype=np.float32)
    onescol = np.ones((128, 128), np.float32)
    onesrow = np.ones((1, 128), np.float32)

    xT = x.T
    in_maps = []
    own_rows = []
    for c in range(NCORE):
        idx = ((8 * np.arange(8)[:, None] + c) * 128
               + np.arange(128)[None, :]).reshape(-1)
        own_rows.append(idx)
        shift = (7 - c) * 128
        xt_c = np.zeros((H, S), np.float32)
        xt_c[:, shift:] = xT[:, :S - shift]
        xo_c = np.ascontiguousarray(xT[:, idx])
        in_maps.append(dict(
            xt=xt_c, xo=xo_c, wq=wq_t, wo=wo_t, wkv=wkv_t, rwt=rwt_t,
            rbias=rb_t, eb=eb_t, btab=build_btab(c), tri=tri, eye=eye,
            onescol=onescol, onesrow=onesrow,
            epsc=np.full((1, 1), EPS, np.float32)))

    if _NC_CACHE is None:
        _NC_CACHE = build_program()
    nc = _NC_CACHE
    res = run_bass_kernel_spmd(nc, in_maps, core_ids=list(range(NCORE)))

    out = np.empty((S, H), np.float32)
    for c in range(NCORE):
        out[own_rows[c]] = res.results[c]["outT"].T
    return out.reshape(1, S, H)
